# version 20
# baseline (speedup 1.0000x reference)
import numpy as np
import ml_dtypes
import concourse.bass as bass
import concourse.mybir as mybir
from concourse.bass_utils import run_bass_kernel_spmd

F32 = mybir.dt.float32
BF16 = mybir.dt.bfloat16
I32 = mybir.dt.int32
AF = mybir.ActivationFunctionType
OP = mybir.AluOpType
AX = mybir.AxisListType

E = 2048
EC = 16
FF = 5504
FFC = 43
TFULL = 8192
TOWN = 4096
ATILE = 512
NAT = TFULL // ATILE
BCH = 1024
NBC = TFULL // BCH


class Sem:
    def __init__(self, nc, name):
        self.h = nc.alloc_semaphore(name)
        self.c = 0


class Prog:
    def __init__(self):
        self.nc = bass.Bass("TRN2", target_bir_lowering=False, detect_race_conditions=False)
        self.streams = {k: [] for k in ("sy", "gp", "ve", "sc", "te")}
        self.sched = {k: [] for k in ("sy", "gp", "ve", "sc", "te")}
        self.sems = {}

    def sem(self, name):
        if name not in self.sems:
            self.sems[name] = Sem(self.nc, name)
        return self.sems[name]

    def emit(self, stream, build_fn, incs=()):
        tgt = [(s.h, n) for s, n in incs]
        for s, n in incs:
            s.c += n

        def run(eng):
            inst = build_fn(eng)
            for h, n in tgt:
                inst.then_inc(h, n)
        self.streams[stream].append(run)
        self.sched[stream].append(("op", [(s, n) for s, n in incs]))

    def wait(self, stream, sem, val=None):
        v = sem.c if val is None else val
        self.streams[stream].append(lambda eng: eng.wait_ge(sem.h, v))
        self.sched[stream].append(("wait", sem, v))

    def dma(self, stream, out_ap, in_ap, incs=(), **kw):
        self.emit(stream, lambda eng: eng.dma_start(out=out_ap, in_=in_ap, **kw), incs)

    def dmat(self, stream, out_ap, in_ap, incs=()):
        self.emit(stream, lambda eng: eng.dma_start_transpose(out=out_ap, in_=in_ap), incs)


def bc16(ap, k, t):
    return ap.rearrange("p (o t) -> p o t", o=1).to_broadcast([128, k, t])


def build_program(debug=False):
    p = Prog()
    nc = p.nc

    xT = nc.dram_tensor("xT", [E, TFULL], F32, kind="ExternalInput").ap()
    wq = nc.dram_tensor("wq", [E, E], F32, kind="ExternalInput").ap()
    wk = nc.dram_tensor("wk", [E, E], F32, kind="ExternalInput").ap()
    wv = nc.dram_tensor("wv", [E, E], F32, kind="ExternalInput").ap()
    wo = nc.dram_tensor("wo", [E, E], F32, kind="ExternalInput").ap()
    lutb_in = nc.dram_tensor("lutb_i", [128, 16], F32, kind="ExternalInput").ap()
    gidxT = nc.dram_tensor("gidxT", [E, FF], I32, kind="ExternalInput").ap()
    uidxT = nc.dram_tensor("uidxT", [E, FF], I32, kind="ExternalInput").ap()
    didxT = nc.dram_tensor("didxT", [FF, E], I32, kind="ExternalInput").ap()
    outT = nc.dram_tensor("outT", [E, TOWN], F32, kind="ExternalOutput").ap()

    OWN0 = 0

    xnfm = nc.dram_tensor("xnfm", [E, TFULL], BF16)
    xntm_d = nc.dram_tensor("xntm_d", [TFULL, E], BF16)
    gdram = nc.dram_tensor("gdram", [E, E], F32)
    wqb_d = nc.dram_tensor("wqb_d", [E, E], BF16)
    wkb_d = nc.dram_tensor("wkb_d", [E, E], BF16)
    wvb_d = nc.dram_tensor("wvb_d", [E, E], BF16)
    wop_d = nc.dram_tensor("wop_d", [EC, 128, E], BF16)

    vfm = nc.dram_tensor("vfm", [E, TOWN], BF16)
    n1d = nc.dram_tensor("n1d", [E, E], BF16)
    wtd = nc.dram_tensor("wtd", [E, E], BF16)
    wgTd = nc.dram_tensor("wgTd", [E, FF], BF16)
    wuTd = nc.dram_tensor("wuTd", [E, FF], BF16)
    wdTp = nc.dram_tensor("wdTp", [EC, 128, FF], BF16)
    hbuf = nc.dram_tensor("hbuf", [FF, TOWN], BF16)
    aofm = nc.dram_tensor("aofm", [E, TOWN], BF16)

    arA = nc.alloc_sbuf_tensor("arA", [128, 32768], BF16)
    arB = nc.alloc_sbuf_tensor("arB", [128, 32768], BF16)
    idxb2 = nc.alloc_sbuf_tensor("idxb2", [128, 4096], I32)
    dqw = nc.alloc_sbuf_tensor("dqw", [128, 8192], BF16)
    stg = nc.alloc_sbuf_tensor("stg", [128, 8192], BF16)
    rows_f = nc.alloc_sbuf_tensor("rows_f", [1, 2048], F32)
    rows_b = nc.alloc_sbuf_tensor("rows_b", [1, 1024], BF16)
    xres = nc.alloc_sbuf_tensor("xres", [128, 2048], F32)
    outstg = nc.alloc_sbuf_tensor("outstg", [128, 2048], F32)
    exps = stg[:, 0:4096]
    cols = nc.alloc_sbuf_tensor("cols", [128, 8], F32)
    smk = nc.alloc_sbuf_tensor("smk", [128, 4], F32)
    sitab = nc.alloc_sbuf_tensor("sitab", [128, 16], F32)
    cst = nc.alloc_sbuf_tensor("cst", [128, 2], F32)
    onescol = nc.alloc_sbuf_tensor("onescol", [128, 2], BF16)
    onesrow = nc.alloc_sbuf_tensor("onesrow", [1, 128], BF16)
    lutb = nc.alloc_sbuf_tensor("lutb", [128, 16], F32)
    absb = nc.alloc_sbuf_tensor("absb", [128, 1024], BF16)
    eyes = nc.alloc_sbuf_tensor("eyes", [128, 128], BF16)
    ps = nc.alloc_psum_tensor("ps", [128, 4096], F32)

    def bank(i, n=1):
        return ps[:, i * 512:(i + n) * 512]

    stgf = stg.bitcast(F32)

    idxb2_bf = idxb2.bitcast(BF16)

    def ao_chunk(ec):
        if ec < 8:
            return dqw[:, ec * 1024:(ec + 1) * 1024]
        return idxb2_bf[:, (ec - 8) * 1024:(ec - 7) * 1024]

    S = p.sem

    eye_s = nc.inline_tensor(
        (np.eye(128) * (0.125 ** 0.5)).astype(ml_dtypes.bfloat16), name="eye_s")

    p.dma("sy", eyes[:, :], eye_s.ap(), incs=[(S("c_ld"), 16)])
    p.dma("sy", lutb[:, :], lutb_in, incs=[(S("c_ld"), 16)])
    p.wait("ve", S("c_ld"), 32)
    p.emit("ve", lambda v: v.memset(onescol[:, :], 1.0))
    p.emit("ve", lambda v: v.memset(onesrow[:, :], 1.0))
    p.emit("ve", lambda v: v.memset(cst[:, 0:1], -32.0))
    p.emit("ve", lambda v: v.memset(cst[:, 1:2], 0.125), incs=[(S("c_init"), 1)])

    dq_tiles = []

    def idx_tile_aps(src, dst, c0, w, k0, nk):
        i_ap = src.rearrange("(k p) o -> p k o", p=128)[:, k0:k0 + nk, c0:c0 + w]
        o_ap = dst.ap().rearrange("(k p) o -> p k o", p=128)[:, k0:k0 + nk, c0:c0 + w]
        return i_ap, o_ap, nk * w

    for og in range(6):
        c0, w = (og * 1024, 1024) if og < 5 else (5120, 384)
        for srcT, dst in ((gidxT, wgTd), (uidxT, wuTd)):
            for g in range(8):
                dq_tiles.append(idx_tile_aps(srcT, dst, c0, w, g * 2, 2))
    for oc in range(EC):
        for k0, nk in ((0, 16), (16, 16), (32, 11)):
            i_ap = didxT.rearrange("(k p) o -> p k o", p=128)[:, k0:k0 + nk, oc * 128:(oc + 1) * 128]
            o_ap = wdTp.ap()[oc][:, k0 * 128:(k0 + nk) * 128]
            dq_tiles.append((i_ap, o_ap, nk * 128))
    N_DQ = len(dq_tiles)

    pc_k, pc_q, pc_v, pc_o = S("pc_k"), S("pc_q"), S("pc_v"), S("pc_o")
    precasts = [(wkb_d.ap(), wk, pc_k), (wqb_d.ap(), wq, pc_q), (wvb_d.ap(), wv, pc_v)] + [
        (wop_d.ap()[oc].rearrange("p (ecc o) -> p ecc o", o=128),
         wo.rearrange("(ecc p) o -> p ecc o", p=128)[:, :, oc * 128:(oc + 1) * 128], pc_o)
        for oc in range(EC)
    ]
    pc_state = {"emitted": 0}

    def pc_pump():
        if pc_state["emitted"] < len(precasts):
            o_ap, i_ap, sem = precasts[pc_state["emitted"]]
            p.dma("gp", o_ap, i_ap, incs=[(sem, 16)])
            pc_state["emitted"] += 1

    dq_state = {"emitted": 0, "loaded": 0}
    dq_ld = [S("dq_ld0"), S("dq_ld1")]
    dq_cv = S("dq_cv")
    dq_ve = S("dq_ve")
    dq_st = [S("dq_st0"), S("dq_st1")]
    dq_slot_st = [[0], [0]]
    dq_ok = S("dq_ok")
    dq_bounds = {16 * (j + 1): j + 1 for j in range(6)}
    dq_bounds.update({96 + 3 * (oc + 1): 7 + oc for oc in range(EC)})
    dq_nst = {"n": 0}

    dq_micro = {"t": 0, "step": 0}

    def dq_pump(n):
        for _ in range(n):
            t = dq_micro["t"]
            if t >= N_DQ:
                return
            step = dq_micro["step"]
            i_ap, o_dst, ncol = dq_tiles[t]
            buf = t % 2
            ibuf = idxb2[:, buf * 2048: buf * 2048 + ncol]
            acc = dqw[:, (2 + buf) * 2048: (2 + buf) * 2048 + ncol]
            idxbf = dqw[:, 0:ncol]
            plane = dqw[:, 2048:2048 + ncol]
            if step == 0:
                ibuf3 = ibuf.rearrange("p (k o) -> p k o", o=i_ap.shape[2])
                if t >= 2:
                    p.wait("gp", dq_cv, t - 1)
                p.dma("gp", ibuf3, i_ap, incs=[(dq_ld[buf], 16)])
                p.wait("ve", dq_ld[buf], 16 * (t // 2 + 1))
                if t >= 2:
                    p.wait("ve", dq_st[buf], 16 * dq_slot_st[buf][0])
                p.emit("ve", lambda v, a=ibuf, b=idxbf: v.tensor_copy(b, a), incs=[(dq_cv, 1)])
                p.wait("ve", dq_cv, t + 1)
                p.emit("ve", lambda v, b=idxbf, a=acc: v.tensor_scalar(
                    a, b, 0.0, lutb[:, 0:1], OP.is_equal, OP.mult))
                dq_micro["step"] = 1
            elif step <= 15:
                l = step
                p.emit("ve", lambda v, b=idxbf, pl=plane, l=l: v.tensor_scalar(
                    pl, b, float(l), lutb[:, l:l + 1], OP.is_equal, OP.mult))
                inc = [(dq_ve, 1)] if l == 15 else []
                p.emit("ve", lambda v, a=acc, pl=plane: v.tensor_tensor(a, a, pl, OP.add), incs=inc)
                dq_micro["step"] = step + 1
            else:
                p.wait("gp", dq_ve, t + 1)
                if len(o_dst.shape) == 3:
                    accv = acc.rearrange("p (k o) -> p k o", o=o_dst.shape[2])
                else:
                    accv = acc
                p.dma("gp", o_dst, accv, incs=[(dq_st[buf], 16)])
                dq_slot_st[buf][0] += 1
                dq_nst["n"] += 1
                if dq_nst["n"] in dq_bounds:
                    p.wait("gp", dq_st[0], 16 * dq_slot_st[0][0])
                    p.wait("gp", dq_st[1], 16 * dq_slot_st[1][0])
                    p.emit("gp", lambda g: g.nop(), incs=[(dq_ok, 1)])
                if (t + 1) % 2 == 0:
                    pc_pump()
                dq_micro["t"] = t + 1
                dq_micro["step"] = 0
                dq_state["emitted"] = t + 1

    def dq_tiles_pump(n):
        dq_pump(n * 17)

    def dq_ok_gateup(og):
        return og + 1


    a_ld = [S("a_ld0"), S("a_ld1")]
    a_sq = S("a_sq")
    a_s1 = S("a_s1")
    a_s2 = S("a_s2")
    a_stats = S("a_stats")
    a_sd = S("a_sd")
    a_rows = S("a_rows")
    a_bc = S("a_bc")
    a_bcrd = S("a_bcrd")
    a_xn = S("a_xn")
    a_xnst = [S("a_xnst0"), S("a_xnst1")]
    a_tp = S("a_tp")
    a_tpe = S("a_tpe")
    a_tmst = [S("a_tmst0"), S("a_tmst1")]
    b_tld = S("b_tld")
    b_chunk = S("b_chunk")
    b_pe = S("b_pe")
    b_ev = S("b_ev")
    b_st = [S("b_st0"), S("b_st1"), S("b_st2"), S("b_st3")]
    vf = S("vf")

    def vfence(inst_builder):
        p.emit("ve", inst_builder, incs=[(vf, 1)])
        p.wait("ve", vf)

    p.wait("te", S("c_init"))
    n_gh = 0
    n_tp = 0

    arAf = arA.bitcast(F32)
    a_cv = S("a_cv")
    for tt in range(NAT):
        q = tt % 2
        tau = tt * ATILE
        xf32h = arAf[:, 0:4096]
        xbf = arA[:, 8192 + q * 8192: 16384 + q * 8192]
        xbf3 = xbf.rearrange("p (k t) -> p k t", t=ATILE)
        sq = arB[:, 0:8192]
        sq3 = sq.rearrange("p (k t) -> p k t", t=ATILE)
        xn = arB[:, 8192:16384] if q == 0 else arA[:, 24576:32768]
        xn3 = xn.rearrange("p (k t) -> p k t", t=ATILE)
        ps_s1 = ps[0:1, 0:512]
        ps_s2 = ps[0:1, 512:1024]
        ps_rb = bank(2)
        ps_mrb = bank(3)
        ps_tp = ps[:, 1024:2048].bitcast(BF16)
        m_row = rows_f[0:1, 0:512]
        v_row = rows_f[0:1, 512:1024]
        sd_row = rows_f[0:1, 1024:1536]
        mr_row = rows_f[0:1, 1536:2048]
        tmp_row = rows_f[0:1, 512:1024]
        rb_row = rows_b[0:1, 0:512]
        mrb_row = rows_b[0:1, 512:1024]
        rbsb = absb[:, 0:512]
        mrbsb = absb[:, 512:1024]

        for hh in range(2):
            if tt + hh > 0:
                p.wait("sy", a_cv, 2 * tt + hh)
            p.dma("sy", xf32h.rearrange("p (k t) -> p k t", t=ATILE),
                  xT.rearrange("(k p) t -> p k t", p=128)[:, hh * 8:(hh + 1) * 8, tau:tau + ATILE],
                  incs=[(a_ld[0], 16)])
            p.wait("sc", a_ld[0], 16 * (2 * tt + hh + 1))
            if hh == 0 and tt >= 2:
                p.wait("sc", a_xn, tt - 1)
            p.emit("sc", lambda s, a=xf32h, b=xbf[:, hh * 4096:(hh + 1) * 4096]:
                   s.copy(b, a), incs=[(a_cv, 1)])
        if tt >= 1:
            p.wait("sc", a_s2, tt)
        p.emit("sc", lambda s, a=xbf, b=sq: s.activation(b, a, AF.Square), incs=[(a_sq, 1)])

        p.wait("te", a_cv, 2 * tt + 2)
        if tt >= 1:
            p.wait("te", a_stats, tt)
        for k in range(EC):
            p.emit("te", lambda t, o=ps_s1, r=xbf3, k=k: t.matmul(
                o, onescol[:, 0:1], r[:, k, :], start=(k == 0), stop=(k == EC - 1)),
                incs=[(a_s1, 1)] if k == EC - 1 else [])
        p.wait("te", a_sq, tt + 1)
        for k in range(EC):
            p.emit("te", lambda t, o=ps_s2, r=sq3, k=k: t.matmul(
                o, onescol[:, 0:1], r[:, k, :], start=(k == 0), stop=(k == EC - 1)),
                incs=[(a_s2, 1)] if k == EC - 1 else [])

        p.wait("ve", a_s1, tt + 1)
        p.wait("ve", a_s2, tt + 1)
        vfence(lambda v, a=ps_s1, o=m_row: v.tensor_scalar(o, a, 1.0 / E, None, OP.mult))
        p.emit("ve", lambda v, a=ps_s2, o=v_row: v.tensor_scalar(o, a, 1.0 / E, 1e-5, OP.mult, OP.add),
               incs=[(a_stats, 1)])
        p.wait("ve", a_stats)
        vfence(lambda v, o=sd_row, a=m_row: v.tensor_tensor(o, a, a, OP.mult))
        p.emit("ve", lambda v, o=v_row, a=sd_row: v.tensor_tensor(o, o, a, OP.subtract),
               incs=[(S("a_var"), 1)])
        p.wait("sc", S("a_var"), tt + 1)
        p.emit("sc", lambda s, o=sd_row, a=v_row: s.activation(o, a, AF.Sqrt),
               incs=[(a_sd, 1)])
        p.wait("ve", a_sd, tt + 1)
        vfence(lambda v, o=tmp_row, a=sd_row: v.reciprocal(o, a))
        vfence(lambda v, o=mr_row, a=m_row, b=tmp_row: v.tensor_tensor(o, a, b, OP.mult))
        vfence(lambda v, o=rb_row, a=tmp_row: v.tensor_copy(o, a))
        p.emit("ve", lambda v, o=mrb_row, a=mr_row: v.tensor_copy(o, a),
               incs=[(a_rows, 1)])
        p.wait("ve", a_rows)

        p.wait("te", a_rows, tt + 1)
        if tt >= 1:
            p.wait("te", a_tpe, n_tp)
        p.emit("te", lambda t, o=ps_rb, r=rb_row: t.matmul(o, onesrow[0:1, :], r, start=True, stop=True))
        p.emit("te", lambda t, o=ps_mrb, r=mrb_row: t.matmul(o, onesrow[0:1, :], r, start=True, stop=True),
               incs=[(a_bc, 1)])

        p.wait("ve", a_bc, tt + 1)
        vfence(lambda v, o=rbsb, a=ps_rb: v.tensor_copy(o, a))
        p.emit("ve", lambda v, o=mrbsb, a=ps_mrb: v.tensor_copy(o, a),
               incs=[(a_bcrd, 1)])
        p.wait("ve", a_bcrd)
        if tt >= 2:
            p.wait("ve", a_tp, 4 * (tt - 1))
            p.wait("ve", a_xnst[q], 16 * ((tt - 2) // 2 + 1))
        vfence(lambda v, o=xn3, a=xbf3, b=rbsb: v.tensor_tensor(
            o, a, bc16(b, EC, ATILE), OP.mult))
        p.emit("ve", lambda v, o=xn3, a=xn3, b=mrbsb:
               v.tensor_tensor(o, a, bc16(b, EC, ATILE), OP.subtract),
               incs=[(a_xn, 1)])

        p.wait("sy", a_xn, tt + 1)
        p.dma("sy", xnfm.ap().rearrange("(k p) t -> p k t", p=128)[:, :, tau:tau + ATILE],
              xn3, incs=[(a_xnst[q], 16)])

        for ts4 in range(4):
            p.wait("te", a_bcrd, tt + 1)
            if n_tp >= 1:
                p.wait("te", a_tpe, n_tp)
            p.wait("te", a_xn, tt + 1)
            for ecc in range(EC):
                p.emit("te", lambda t, o=ps_tp[:, ecc * 128:(ecc + 1) * 128],
                       i=xn3[:, ecc, ts4 * 128:(ts4 + 1) * 128]:
                       t.transpose(o, i, eyes[:, :]),
                       incs=[(a_tp, 1)] if ecc == EC - 1 else [])
            p.wait("ve", a_tp, n_tp + 1)
            tpslot = outstg.bitcast(BF16)[:, (n_tp % 2) * 2048:(n_tp % 2) * 2048 + 2048]
            if n_tp >= 2:
                p.wait("ve", a_tmst[n_tp % 2], 16 * ((n_tp - 2) // 2 + 1))
            p.emit("ve", lambda v, o=tpslot, a=ps_tp: v.tensor_scalar(
                o, a, 0.125 ** 0.5, None, OP.mult), incs=[(a_tpe, 1)])
            p.wait("sy", a_tpe, n_tp + 1)
            p.dma("sy", xntm_d.ap()[tau + ts4 * 128: tau + (ts4 + 1) * 128, :],
                  tpslot, incs=[(a_tmst[n_tp % 2], 16)])
            n_tp += 1

        dq_pump(6)

        if tt % 2 == 1:
            c = tt // 2
            btile = arB[:, 16384:32768]
            bt3 = btile.rearrange("p (n e) -> p n e", e=2048)
            p.wait("sy", a_tmst[0], 16 * 2 * (tt + 1))
            p.wait("sy", a_tmst[1], 16 * 2 * (tt + 1))
            if c >= 1:
                p.wait("sy", b_chunk, c)
            p.dma("sy", bt3, xntm_d.ap().rearrange("(n p) e -> p n e", p=128)[:, 8 * c:8 * c + 8, :],
                  incs=[(b_tld, 16)])
            p.wait("te", b_tld, 16 * (c + 1))
            for eg in range(EC):
                for half in range(2):
                    slot = 4 + 2 * (n_gh % 2)
                    if n_gh >= 2:
                        p.wait("te", b_ev, n_gh - 1)
                    for ts in range(8):
                        lhsT = bt3[:, ts, eg * 128:(eg + 1) * 128]
                        for j in range(2):
                            out = ps[:, slot * 512 + j * 512: slot * 512 + (j + 1) * 512]
                            rhs = bt3[:, ts, half * 1024 + j * 512: half * 1024 + (j + 1) * 512]
                            p.emit("te", lambda t, o=out, l=lhsT, r=rhs, s=(ts == 0), e=(ts == 7):
                                   t.matmul(o, l, r, start=s, stop=e),
                                   incs=[(b_pe, 1)] if (ts == 7 and j == 1) else [])
                    p.wait("sc", b_pe, n_gh + 1)
                    sslot = stgf[:, (n_gh % 4) * 1024:(n_gh % 4) * 1024 + 1024]
                    if n_gh >= 4:
                        p.wait("sc", b_st[n_gh % 4], 16 * ((n_gh - 4) // 4 + 1))
                    p.emit("sc", lambda s, o=sslot, a=ps[:, slot * 512: slot * 512 + 1024]:
                           s.copy(o, a), incs=[(b_ev, 1)])
                    p.wait("gp", b_ev, n_gh + 1)
                    kw = {} if c == 0 else {"accum_op": OP.add}
                    p.dma("gp", gdram.ap().rearrange("(k p) t -> p k t", p=128)[
                        :, eg:eg + 1, half * 1024:(half + 1) * 1024],
                        sslot.rearrange("p (o t) -> p o t", o=1),
                        incs=[(b_st[n_gh % 4], 16)], **kw)
                    n_gh += 1
                dq_pump(3)
            p.emit("te", lambda t: t.nop(), incs=[(b_chunk, 1)])

    c_wk = S("c_wk")
    c_gk = S("c_gk")
    c_pe = S("c_pe")
    c_ev = S("c_ev")
    c_st = [S("c_st0"), S("c_st1")]

    for bs in b_st:
        p.wait("gp", bs, 16 * 64)
    for kc in range(EC):
        p.dma("gp", arB[:, kc * 2048:(kc + 1) * 2048].rearrange("p (o t) -> p o t", o=1),
              gdram.ap().rearrange("(k p) t -> p k t", p=128)[:, kc:kc + 1, :], incs=[(c_gk, 16)])
    p.wait("sy", b_chunk, NBC)
    p.wait("sy", a_xnst[0], 16 * 8)
    p.wait("sy", a_xnst[1], 16 * 8)
    p.wait("sy", pc_k, 16)
    p.dma("sy", arA.rearrange("p (k t) -> p k t", t=2048),
          wkb_d.ap().rearrange("(k p) t -> p k t", p=128), incs=[(c_wk, 16)])

    for ec in range(EC):
        q = ec % 2
        pbank = 0 if q == 0 else 4
        p.wait("te", c_wk, 16)
        p.wait("te", c_gk, 16 * EC)
        if ec >= 2:
            p.wait("te", c_ev, ec - 1)
        for fc in range(EC):
            lhsT = arB[:, fc * 2048 + ec * 128: fc * 2048 + (ec + 1) * 128]
            for j in range(4):
                rhs = arA[:, fc * 2048 + j * 512: fc * 2048 + (j + 1) * 512]
                out = ps[:, pbank * 512 + j * 512: pbank * 512 + (j + 1) * 512]
                p.emit("te", lambda t, o=out, l=lhsT, r=rhs, s=(fc == 0), e=(fc == EC - 1):
                       t.matmul(o, l, r, start=s, stop=e),
                       incs=[(c_pe, 1)] if (fc == EC - 1 and j == 3) else [])
        p.wait("sc", c_pe, ec + 1)
        n1stg = stg[:, (ec % 2) * 2048:(ec % 2) * 2048 + 2048]
        if ec >= 2:
            p.wait("sc", c_st[ec % 2], 16 * ((ec - 2) // 2 + 1))
        p.emit("sc", lambda s, o=n1stg, a=ps[:, pbank * 512: pbank * 512 + 2048]:
               s.copy(o, a), incs=[(c_ev, 1)])
        p.wait("sy", c_ev, ec + 1)
        p.dma("sy", n1d.ap().rearrange("(k p) t -> p k t", p=128)[:, ec:ec + 1, :],
              n1stg.rearrange("p (o t) -> p o t", o=1), incs=[(c_st[ec % 2], 16)])
        dq_pump(20)

    s_wq = S("s_wq")
    s_n1 = S("s_n1")
    s_pe = S("s_pe")
    s_nm = S("s_nm")
    s_cp = S("s_cp")
    s_exp = S("s_exp")
    s_wt = S("s_wt")
    s_si = S("s_si")
    s_wtst = [S("s_wtst0"), S("s_wtst1")]

    p.wait("sy", c_pe, EC)
    p.wait("sy", pc_q, 16)
    p.dma("sy", arA.rearrange("p (k t) -> p k t", t=2048),
          wqb_d.ap().rearrange("(k p) t -> p k t", p=128), incs=[(s_wq, 16)])
    for kc in range(EC):
        p.wait("sy", c_pe, EC)
        p.wait("sy", c_st[0], 16 * 8)
        p.wait("sy", c_st[1], 16 * 8)
        p.dma("sy", arB[:, kc * 2048:(kc + 1) * 2048].rearrange("p (o t) -> p o t", o=1),
              n1d.ap().rearrange("(k p) t -> p k t", p=128)[:, kc:kc + 1, :], incs=[(s_n1, 16)])

    for ic in range(EC):
        q = ic % 2
        pbank = 0 if q == 0 else 4
        p.wait("te", s_wq, 16)
        p.wait("te", s_n1, 16 * EC)
        if ic >= 2:
            p.wait("te", s_cp, ic - 1)
        for ecc in range(EC):
            lhsT = arA[:, ecc * 2048 + ic * 128: ecc * 2048 + (ic + 1) * 128]
            for j in range(4):
                rhs = arB[:, ecc * 2048 + j * 512: ecc * 2048 + (j + 1) * 512]
                out = ps[:, pbank * 512 + j * 512: pbank * 512 + (j + 1) * 512]
                p.emit("te", lambda t, o=out, l=lhsT, r=rhs, s=(ecc == 0), e=(ecc == EC - 1):
                       t.matmul(o, l, r, start=s, stop=e),
                       incs=[(s_pe, 1)] if (ecc == EC - 1 and j == 3) else [])
        psc = ps[:, pbank * 512: pbank * 512 + 2048]
        sslot = (outstg if ic % 2 == 0 else xres)[:, 0:2048]
        cm = cols[:, q * 4 + 0: q * 4 + 1]
        nm = cols[:, q * 4 + 1: q * 4 + 2]
        sm = smk[:, ic % 4: ic % 4 + 1]
        si = sitab[:, ic: ic + 1]
        ex = exps[:, q * 2048:(q + 1) * 2048]
        p.wait("sc", s_pe, ic + 1)
        if ic >= 2:
            p.wait("sc", s_exp, ic - 1)
        p.emit("sc", lambda s, o=sslot, a=psc, c=cm: s.activation(o, a, AF.Copy, accum_out=c),
               incs=[(s_cp, 1)])
        p.wait("gp", s_cp, ic + 1)
        p.emit("gp", lambda g, o=nm, a=cm: g.tensor_scalar(
            o, a, -1.0 / 2048.0, -95.0, OP.mult, OP.add), incs=[(s_nm, 1)])
        p.wait("sc", s_nm, ic + 1)
        if ic >= 2:
            p.wait("sc", s_wtst[ic % 2], 16 * ((ic - 2) // 2 + 1))
        if ic >= 4:
            p.wait("sc", s_si, ic - 3)
        p.emit("sc", lambda s, o=ex, a=sslot, b=nm, c=sm: s.activation(
            o, a, AF.Exp, bias=b, accum_out=c), incs=[(s_exp, 1)])
        p.wait("gp", s_exp, ic + 1)
        p.dma("gp", wtd.ap().rearrange("(k p) t -> p k t", p=128)[:, ic:ic + 1, :],
              ex.rearrange("p (o t) -> p o t", o=1), incs=[(s_wtst[ic % 2], 16)])
        p.wait("ve", s_exp, ic + 1)
        p.emit("ve", lambda v, o=si, a=sm: v.reciprocal(o, a), incs=[(s_si, 1)])
        dq_pump(8)

    v_wv = S("v_wv")
    v_xld = [S("v_xld0"), S("v_xld1")]
    v_pe = S("v_pe")
    v_ev = S("v_ev")
    v_st = S("v_st")

    p.wait("sy", s_pe, EC)
    p.wait("sy", pc_v, 16)
    p.dma("sy", arA.rearrange("p (k t) -> p k t", t=2048),
          wvb_d.ap().rearrange("(k p) t -> p k t", p=128), incs=[(v_wv, 16)])

    VT = 1024
    for vt in range(TOWN // VT):
        q = vt % 2
        xnt = arB[:, q * 16384:(q + 1) * 16384]
        xnt3 = xnt.rearrange("p (k t) -> p k t", t=VT)
        p.wait("sy", s_pe, EC)
        if vt >= 2:
            p.wait("sy", v_pe, (vt - 1) * EC)
        p.dma("sy", xnt3,
              xnfm.ap().rearrange("(k p) t -> p k t", p=128)[:, :, OWN0 + vt * VT: OWN0 + (vt + 1) * VT],
              incs=[(v_xld[q], 16)])
        p.wait("te", v_wv, 16)
        p.wait("te", v_xld[q], 16 * (vt // 2 + 1))
        for fc in range(EC):
            pbank = (fc % 4) * 2
            if vt * EC + fc >= 4:
                p.wait("te", v_ev, vt * EC + fc - 3)
            for ecc in range(EC):
                lhsT = arA[:, ecc * 2048 + fc * 128: ecc * 2048 + (fc + 1) * 128]
                for j in range(2):
                    out = ps[:, pbank * 512 + j * 512: pbank * 512 + (j + 1) * 512]
                    p.emit("te", lambda t, o=out, l=lhsT, r=xnt3[:, ecc, j * 512:(j + 1) * 512],
                           s=(ecc == 0), e=(ecc == EC - 1): t.matmul(o, l, r, start=s, stop=e),
                           incs=[(v_pe, 1)] if (ecc == EC - 1 and j == 1) else [])
            p.wait("sc", v_pe, vt * EC + fc + 1)
            slot = stg[:, (fc % 8) * 1024:(fc % 8 + 1) * 1024]
            if vt == 0 and fc < 4:
                p.wait("sc", s_wtst[0], 16 * 8)
                p.wait("sc", s_wtst[1], 16 * 8)
            if vt * EC + fc >= 8:
                p.wait("sc", v_st, 16 * ((vt * EC + fc - 7 + 7) // 8))
            p.emit("sc", lambda s, o=slot, a=ps[:, pbank * 512: pbank * 512 + 1024]:
                   s.copy(o, a), incs=[(v_ev, 1)])
            if fc % 8 == 7:
                p.wait("sc", v_ev, vt * EC + fc + 1)
                h8 = fc // 8
                p.dma("sc", vfm.ap().rearrange("(g p) t -> p g t", p=128)[:, h8 * 8:(h8 + 1) * 8, vt * VT:(vt + 1) * VT],
                      stg[:, 0:8192].rearrange("p (g t) -> p g t", t=VT), incs=[(v_st, 16)])
            dq_pump(7)

    o_wt = S("o_wt")
    o_vld = [S("o_vld0"), S("o_vld1")]
    o_pe = S("o_pe")
    o_ev = S("o_ev")
    o_st = S("o_st")

    for fc in range(EC):
        p.wait("sy", v_pe, (TOWN // VT) * EC)
        p.wait("sy", s_wtst[0], 16 * 8)
        p.wait("sy", s_wtst[1], 16 * 8)
        p.dmat("sy", arA[:, fc * 2048:(fc + 1) * 2048], wtd.ap()[:, fc * 128:(fc + 1) * 128],
               incs=[(o_wt, 16)])

    for vt in range(TOWN // VT):
        q = vt % 2
        vtile = arB[:, q * 16384:(q + 1) * 16384]
        vtile3 = vtile.rearrange("p (k t) -> p k t", t=VT)
        p.wait("sy", v_st, 16 * 2 * (TOWN // VT))
        if vt >= 2:
            p.wait("sy", o_pe, (vt - 1) * EC)
        p.dma("sy", vtile3,
              vfm.ap().rearrange("(k p) t -> p k t", p=128)[:, :, vt * VT:(vt + 1) * VT],
              incs=[(o_vld[q], 16)])
        p.wait("te", o_wt, 16 * EC)
        p.wait("te", o_vld[q], 16 * (vt // 2 + 1))
        for ecc in range(EC):
            pbank = (ecc % 4) * 2
            if vt * EC + ecc >= 4:
                p.wait("te", o_ev, vt * EC + ecc - 3)
            for fc in range(EC):
                lhsT = arA[:, fc * 2048 + ecc * 128: fc * 2048 + (ecc + 1) * 128]
                for j in range(2):
                    out = ps[:, pbank * 512 + j * 512: pbank * 512 + (j + 1) * 512]
                    p.emit("te", lambda t, o=out, l=lhsT, r=vtile3[:, fc, j * 512:(j + 1) * 512],
                           s=(fc == 0), e=(fc == EC - 1): t.matmul(o, l, r, start=s, stop=e),
                           incs=[(o_pe, 1)] if (fc == EC - 1 and j == 1) else [])
            p.wait("ve", o_pe, vt * EC + ecc + 1)
            slot = stg[:, (ecc % 8) * 1024:(ecc % 8 + 1) * 1024]
            if vt * EC + ecc >= 8:
                p.wait("ve", o_st, 16 * ((vt * EC + ecc - 7 + 7) // 8))
            p.emit("ve", lambda v, o=slot, a=ps[:, pbank * 512: pbank * 512 + 1024],
                   b=sitab[:, ecc:ecc + 1]: v.tensor_scalar(o, a, b, None, OP.mult),
                   incs=[(o_ev, 1)])
            if ecc % 8 == 7:
                p.wait("sc", o_ev, vt * EC + ecc + 1)
                h8 = ecc // 8
                p.dma("sc", aofm.ap().rearrange("(g p) t -> p g t", p=128)[:, h8 * 8:(h8 + 1) * 8, vt * VT:(vt + 1) * VT],
                      stg[:, 0:8192].rearrange("p (g t) -> p g t", t=VT), incs=[(o_st, 16)])
            dq_pump(7)

    g_xld = S("g_xld")
    g_pang = [S("g_pang0"), S("g_pang1")]
    g_panu = [S("g_panu0"), S("g_panu1")]
    g_gate = S("g_gate")
    g_si = S("g_si")
    g_up = S("g_up")
    g_h = S("g_h")
    g_hst = S("g_hst")
    g_sg = S("g_sg")

    GUT = 2048
    p.wait("sc", o_st, 16 * 2 * (TOWN // VT))
    p.wait("ve", o_st, 16 * 2 * (TOWN // VT))
    subgroups = [(i * 4, 4) for i in range(10)] + [(40, 3)]
    pan_w = 512

    npan = 0
    n_o = 0
    for tb in range(TOWN // GUT):
        xnt3 = arA.rearrange("p (k t) -> p k t", t=GUT)
        if tb == 0:
            p.wait("sy", o_pe, (TOWN // VT) * EC)
        else:
            p.wait("sy", g_sg, len(subgroups) * tb)
        p.dma("sy", xnt3,
              xnfm.ap().rearrange("(k p) t -> p k t", p=128)[:, :, OWN0 + tb * GUT: OWN0 + (tb + 1) * GUT],
              incs=[(g_xld, 16)])
        for sgi, (o0, no) in enumerate(subgroups):
            sgq = sgi % 2
            wcols = no * 128
            gp_ap = arB[:, sgq * 16384: sgq * 16384 + 16 * pan_w]
            up_ap = arB[:, sgq * 16384 + 16 * pan_w: sgq * 16384 + 32 * pan_w]
            og_hi = ((o0 + no) * 128 - 1) // 1024
            glob_sg = tb * len(subgroups) + sgi
            for w_dram, pan, psem in ((wgTd, gp_ap, g_pang[sgq]), (wuTd, up_ap, g_panu[sgq])):
                p.wait("sy", dq_ok, dq_ok_gateup(og_hi))
                if glob_sg >= 2:
                    p.wait("sy", g_sg, glob_sg - 1)
                p.dma("sy", pan.rearrange("p (k w) -> p k w", w=pan_w)[:, :, 0:wcols],
                      w_dram.ap().rearrange("(k p) o -> p k o", p=128)[:, :, o0 * 128:(o0 + no) * 128],
                      incs=[(psem, 16)])
            p.wait("te", g_pang[sgq])
            p.wait("te", g_panu[sgq])
            p.wait("te", g_xld, 16 * (tb + 1))
            for oi in range(no):
                for th in range(2):
                    gslot = (n_o % 2) * 2
                    uslot = 4 + (n_o % 2) * 2
                    if n_o >= 2:
                        p.wait("te", g_si, n_o - 1)
                    for hc in range(EC):
                        lhsT = gp_ap[:, hc * pan_w + oi * 128: hc * pan_w + (oi + 1) * 128]
                        for j in range(2):
                            out = ps[:, gslot * 512 + j * 512: gslot * 512 + (j + 1) * 512]
                            p.emit("te", lambda t, o=out, l=lhsT,
                                   r=xnt3[:, hc, th * 1024 + j * 512: th * 1024 + (j + 1) * 512],
                                   s=(hc == 0), e=(hc == EC - 1): t.matmul(o, l, r, start=s, stop=e),
                                   incs=[(g_gate, 1)] if (hc == EC - 1 and j == 1) else [])
                    p.wait("sc", g_gate, n_o + 1)
                    silu_sb = stg[:, (n_o % 2) * 1024: (n_o % 2) * 1024 + 1024]
                    if n_o >= 2:
                        p.wait("sc", g_h, n_o - 1)
                    p.emit("sc", lambda s, o=silu_sb, a=ps[:, gslot * 512: gslot * 512 + 1024]:
                           s.activation(o, a, AF.Silu), incs=[(g_si, 1)])
                    if n_o >= 2:
                        p.wait("te", g_h, n_o - 1)
                    for hc in range(EC):
                        lhsT = up_ap[:, hc * pan_w + oi * 128: hc * pan_w + (oi + 1) * 128]
                        for j in range(2):
                            out = ps[:, uslot * 512 + j * 512: uslot * 512 + (j + 1) * 512]
                            p.emit("te", lambda t, o=out, l=lhsT,
                                   r=xnt3[:, hc, th * 1024 + j * 512: th * 1024 + (j + 1) * 512],
                                   s=(hc == 0), e=(hc == EC - 1): t.matmul(o, l, r, start=s, stop=e),
                                   incs=[(g_up, 1)] if (hc == EC - 1 and j == 1) else [])
                    p.wait("ve", g_up, n_o + 1)
                    p.wait("ve", g_si, n_o + 1)
                    h_sb = stg[:, 2048 + (n_o % 2) * 1024: 2048 + (n_o % 2) * 1024 + 1024]
                    if n_o >= 2:
                        p.wait("ve", g_hst, 16 * (n_o - 1))
                    p.emit("ve", lambda v, o=h_sb, a=silu_sb, b=ps[:, uslot * 512: uslot * 512 + 1024]:
                           v.tensor_tensor(o, a, b, OP.mult), incs=[(g_h, 1)])
                    p.wait("sc", g_h, n_o + 1)
                    o_glob = o0 + oi
                    p.dma("sc", hbuf.ap()[o_glob * 128:(o_glob + 1) * 128,
                                          tb * GUT + th * 1024: tb * GUT + (th + 1) * 1024],
                          h_sb, incs=[(g_hst, 16)])
                    n_o += 1
                dq_pump(10)
            p.emit("te", lambda t: t.nop(), incs=[(g_sg, 1)])

    dq_pump(10 ** 6)

    f_h = S("f_h")
    f_ao = S("f_ao")
    f_wd = [S("f_wd0"), S("f_wd1"), S("f_wd2")]
    f_wo = [S("f_wo0"), S("f_wo1"), S("f_wo2")]
    f_x = [S("f_x0"), S("f_x1")]
    f_pe = S("f_pe")
    f_ev = S("f_ev")
    f_out = [S("f_out0"), S("f_out1")]
    f_tb = S("f_tb")

    FT = 1024
    nf = 0
    for tb in range(TOWN // FT):
        q = tb % 2
        p.wait("sy", g_hst, 16 * 172)
        if tb >= 1:
            p.wait("sy", f_tb, tb)
        p.dma("sy", arA.rearrange("p (k t) -> p k t", t=FT),
              hbuf.ap().rearrange("(k p) t -> p k t", p=128)[:, 0:32, tb * FT:(tb + 1) * FT],
              incs=[(f_h, 16)])
        p.dma("sy", arB[:, 0:9216].rearrange("p (k t) -> p k t", t=FT),
              hbuf.ap().rearrange("(k p) t -> p k t", p=128)[:, 32:41, tb * FT:(tb + 1) * FT],
              incs=[(f_h, 16)])
        p.dma("sy", stg[:, 0:2048].rearrange("p (k t) -> p k t", t=FT),
              hbuf.ap().rearrange("(k p) t -> p k t", p=128)[:, 41:43, tb * FT:(tb + 1) * FT],
              incs=[(f_h, 16)])
        p.wait("gp", dq_ok, 22)
        p.wait("gp", o_st, 16 * 2 * (TOWN // VT))
        p.wait("gp", g_hst, 16 * 172)
        p.wait("gp", pc_o, 16 * 16)
        if tb >= 1:
            p.wait("gp", f_tb, tb)
        p.dma("gp", dqw.rearrange("p (k t) -> p k t", t=1024),
              aofm.ap().rearrange("(k p) t -> p k t", p=128)[:, 0:8, tb * FT:(tb + 1) * FT],
              incs=[(f_ao, 16)])
        p.dma("gp", idxb2_bf[:, 0:8192].rearrange("p (k t) -> p k t", t=1024),
              aofm.ap().rearrange("(k p) t -> p k t", p=128)[:, 8:16, tb * FT:(tb + 1) * FT],
              incs=[(f_ao, 16)])

        def h_chunk(hc, tbq=0):
            if hc < 32:
                return arA[:, hc * FT:(hc + 1) * FT]
            if hc < 41:
                return arB[:, (hc - 32) * FT:(hc - 31) * FT]
            return stg[:, (hc - 41) * FT:(hc - 40) * FT]

        for oc in range(EC):
            oq = nf % 2
            o3 = nf % 3
            wd_pan = arB[:, 9216 + o3 * 5504: 9216 + o3 * 5504 + 5504]
            wo_pan = arB[:, 25728 + o3 * 2048: 25728 + o3 * 2048 + 2048]
            p.wait("sy", dq_ok, 7 + oc)
            if nf >= 3:
                p.wait("sy", f_pe, nf - 2)
            p.dma("sy", wd_pan, wdTp.ap()[oc], incs=[(f_wd[o3], 16)])
            p.wait("gp", f_ao, 16 * 2 * (tb + 1))
            if nf >= 3:
                p.wait("gp", f_pe, nf - 2)
            p.dma("gp", wo_pan, wop_d.ap()[oc], incs=[(f_wo[o3], 16)])
            xr = xres[:, oq * 1024: oq * 1024 + 1024]
            if nf >= 2:
                p.wait("sc", f_ev, nf - 1)
            p.dma("sc", xr.rearrange("p (o t) -> p o t", o=1),
                  xT.rearrange("(k p) t -> p k t", p=128)[:, oc:oc + 1, OWN0 + tb * FT: OWN0 + (tb + 1) * FT],
                  incs=[(f_x[oq], 16)])
            pbank = (nf % 2) * 2
            if nf < 2:
                p.wait("te", g_si, 172)
                p.wait("te", g_h, 172)
            p.wait("te", f_wd[o3], 16 * (nf // 3 + 1))
            p.wait("te", f_wo[o3], 16 * (nf // 3 + 1))
            p.wait("te", f_h, 16 * 3 * (tb + 1))
            p.wait("te", f_ao, 16 * 2 * (tb + 1))
            if nf >= 2:
                p.wait("te", f_ev, nf - 1)
            nmm = 0
            for hc in range(FFC):
                lhsT = wd_pan[:, hc * 128:(hc + 1) * 128]
                rhs_t = h_chunk(hc)
                for j in range(2):
                    out = ps[:, pbank * 512 + j * 512: pbank * 512 + (j + 1) * 512]
                    p.emit("te", lambda t, o=out, l=lhsT, r=rhs_t[:, j * 512:(j + 1) * 512],
                           s=(hc == 0): t.matmul(o, l, r, start=s, stop=False))
                    nmm += 1
            for ecc in range(EC):
                lhsT = wo_pan[:, ecc * 128:(ecc + 1) * 128]
                rhs_t = ao_chunk(ecc)
                for j in range(2):
                    out = ps[:, pbank * 512 + j * 512: pbank * 512 + (j + 1) * 512]
                    last = (ecc == EC - 1)
                    p.emit("te", lambda t, o=out, l=lhsT, r=rhs_t[:, j * 512:(j + 1) * 512],
                           e=last: t.matmul(o, l, r, start=False, stop=e),
                           incs=[(f_pe, 1)] if (last and j == 1) else [])
            p.wait("ve", f_pe, nf + 1)
            p.wait("ve", f_x[oq], 16 * (nf // 2 + 1))
            ostg = outstg[:, oq * 1024: oq * 1024 + 1024]
            if nf >= 2:
                p.wait("ve", f_out[oq], 16 * ((nf - 2) // 2 + 1))
            p.emit("ve", lambda v, o=ostg, a=ps[:, pbank * 512: pbank * 512 + 1024], b=xr:
                   v.tensor_tensor(o, a, b, OP.add), incs=[(f_ev, 1)])
            p.wait("sc", f_ev, nf + 1)
            p.dma("sc", outT.rearrange("(k p) t -> p k t", p=128)[:, oc:oc + 1, tb * FT:(tb + 1) * FT],
                  ostg.rearrange("p (o t) -> p o t", o=1), incs=[(f_out[oq], 16)])
            nf += 1
        p.emit("te", lambda t: t.nop(), incs=[(f_tb, 1)])

    assert dq_state["emitted"] == N_DQ, f"dequant tiles not all emitted: {dq_state['emitted']}/{N_DQ}"

    if debug:
        dbg = {}
        p.wait("gp", f_out[0], 16 * 32)
        p.wait("gp", f_out[1], 16 * 32)
        for name, t, shape in (("d_xnfm", xnfm, [E, TFULL]), ("d_n1", n1d, [E, E]),
                               ("d_wt", wtd, [E, E]), ("d_vfm", vfm, [E, TOWN]),
                               ("d_h", hbuf, [FF, TOWN]), ("d_ao", aofm, [E, TOWN]),
                               ("d_wg", wgTd, [E, FF]), ("d_wd", wdTp, [EC, 128, FF])):
            o = nc.dram_tensor(name, shape, BF16, kind="ExternalOutput").ap()
            p.dma("gp", o, t.ap(), incs=[(S("dbg_st"), 16)])
        p.wait("gp", S("dbg_st"), S("dbg_st").c)

    p.wait("gp", f_out[0], 16 * 32)
    p.wait("gp", f_out[1], 16 * 32)

    with nc.Block() as block:
        @block.sync
        def _(eng):
            for f in p.streams["sy"]:
                f(eng)

        @block.gpsimd
        def _(eng):
            for f in p.streams["gp"]:
                f(eng)

        @block.vector
        def _(eng):
            for f in p.streams["ve"]:
                f(eng)

        @block.scalar
        def _(eng):
            for f in p.streams["sc"]:
                f(eng)

        @block.tensor
        def _(eng):
            for f in p.streams["te"]:
                f(eng)

    return nc, p


_cache = {}


def kernel(L0_attn_x, L0_attn_Wq, L0_attn_Wk, L0_attn_Wv, L0_attn_Wo,
           L0_ffn_lut, L0_ffn_gate_idx, L0_ffn_up_idx, L0_ffn_down_idx):
    x = np.asarray(L0_attn_x)
    B, S, Emod = x.shape
    assert (B, S, Emod) == (4, 8192, 2048)
    lut = np.asarray(L0_ffn_lut).astype(np.float32).reshape(16)
    lutb = np.ascontiguousarray(np.broadcast_to(lut, (128, 16))).astype(np.float32)
    gidxT = np.ascontiguousarray(np.asarray(L0_ffn_gate_idx).T).astype(np.int32)
    uidxT = np.ascontiguousarray(np.asarray(L0_ffn_up_idx).T).astype(np.int32)
    didxT = np.ascontiguousarray(np.asarray(L0_ffn_down_idx).T).astype(np.int32)
    wqa = np.ascontiguousarray(np.asarray(L0_attn_Wq)).astype(np.float32)
    wka = np.ascontiguousarray(np.asarray(L0_attn_Wk)).astype(np.float32)
    wva = np.ascontiguousarray(np.asarray(L0_attn_Wv)).astype(np.float32)
    woa = np.ascontiguousarray(np.asarray(L0_attn_Wo)).astype(np.float32)

    if "nc" not in _cache:
        _cache["nc"] = build_program()[0]
    nc = _cache["nc"]

    in_maps = []
    for c in range(8):
        b, h = c // 2, c % 2
        xb = x[b].T.astype(np.float32)
        if h == 1:
            xb = np.concatenate([xb[:, 4096:], xb[:, :4096]], axis=1)
        in_maps.append(dict(
            xT=np.ascontiguousarray(xb), wq=wqa, wk=wka, wv=wva, wo=woa,
            lutb_i=lutb, gidxT=gidxT, uidxT=uidxT, didxT=didxT,
        ))
    res = run_bass_kernel_spmd(nc, in_maps, core_ids=list(range(8)))
    out = np.empty((4, 8192, 2048), dtype=np.float32)
    for c in range(8):
        b, h = c // 2, c % 2
        out[b, h * 4096:(h + 1) * 4096, :] = res.results[c]["outT"].T
    return out


# version 22 (final)
# speedup vs baseline: 1.1072x; 1.1072x over previous
import numpy as np
import ml_dtypes
import concourse.bass as bass
import concourse.mybir as mybir
from concourse.bass_utils import run_bass_kernel_spmd

F32 = mybir.dt.float32
BF16 = mybir.dt.bfloat16
I32 = mybir.dt.int32
AF = mybir.ActivationFunctionType
OP = mybir.AluOpType
AX = mybir.AxisListType

E = 2048
EC = 16
FF = 5504
FFC = 43
TFULL = 8192
TOWN = 4096
ATILE = 512
NAT = TFULL // ATILE
BCH = 1024
NBC = TFULL // BCH


class Sem:
    def __init__(self, nc, name):
        self.h = nc.alloc_semaphore(name)
        self.c = 0


class Prog:
    def __init__(self):
        self.nc = bass.Bass("TRN2", target_bir_lowering=False, detect_race_conditions=False)
        self.streams = {k: [] for k in ("sy", "gp", "ve", "sc", "te")}
        self.sched = {k: [] for k in ("sy", "gp", "ve", "sc", "te")}
        self.sems = {}

    def sem(self, name):
        if name not in self.sems:
            self.sems[name] = Sem(self.nc, name)
        return self.sems[name]

    def emit(self, stream, build_fn, incs=()):
        tgt = [(s.h, n) for s, n in incs]
        for s, n in incs:
            s.c += n

        def run(eng):
            inst = build_fn(eng)
            for h, n in tgt:
                inst.then_inc(h, n)
        self.streams[stream].append(run)
        self.sched[stream].append(("op", [(s, n) for s, n in incs]))

    def wait(self, stream, sem, val=None):
        v = sem.c if val is None else val
        self.streams[stream].append(lambda eng: eng.wait_ge(sem.h, v))
        self.sched[stream].append(("wait", sem, v))

    def dma(self, stream, out_ap, in_ap, incs=(), **kw):
        self.emit(stream, lambda eng: eng.dma_start(out=out_ap, in_=in_ap, **kw), incs)

    def dmat(self, stream, out_ap, in_ap, incs=()):
        self.emit(stream, lambda eng: eng.dma_start_transpose(out=out_ap, in_=in_ap), incs)


def bc16(ap, k, t):
    return ap.rearrange("p (o t) -> p o t", o=1).to_broadcast([128, k, t])


def build_program(debug=False):
    p = Prog()
    nc = p.nc

    xT = nc.dram_tensor("xT", [E, TFULL], F32, kind="ExternalInput").ap()
    wq = nc.dram_tensor("wq", [E, E], F32, kind="ExternalInput").ap()
    wk = nc.dram_tensor("wk", [E, E], F32, kind="ExternalInput").ap()
    wv = nc.dram_tensor("wv", [E, E], F32, kind="ExternalInput").ap()
    wo = nc.dram_tensor("wo", [E, E], F32, kind="ExternalInput").ap()
    lutb_in = nc.dram_tensor("lutb_i", [128, 16], F32, kind="ExternalInput").ap()
    gidxT = nc.dram_tensor("gidxT", [E, FF], I32, kind="ExternalInput").ap()
    uidxT = nc.dram_tensor("uidxT", [E, FF], I32, kind="ExternalInput").ap()
    didxT = nc.dram_tensor("didxT", [FF, E], I32, kind="ExternalInput").ap()
    outT = nc.dram_tensor("outT", [E, TOWN], F32, kind="ExternalOutput").ap()

    OWN0 = 0

    xnfm = nc.dram_tensor("xnfm", [E, TFULL], BF16)
    xntm_d = nc.dram_tensor("xntm_d", [TFULL, E], BF16)
    gdram = nc.dram_tensor("gdram", [E, E], F32)
    wqb_d = nc.dram_tensor("wqb_d", [E, E], BF16)
    wkb_d = nc.dram_tensor("wkb_d", [E, E], BF16)
    wvb_d = nc.dram_tensor("wvb_d", [E, E], BF16)
    wop_d = nc.dram_tensor("wop_d", [EC, 128, E], BF16)

    vfm = nc.dram_tensor("vfm", [E, TOWN], BF16)
    n1d = nc.dram_tensor("n1d", [E, E], BF16)
    wtd = nc.dram_tensor("wtd", [E, E], BF16)
    wgTd = nc.dram_tensor("wgTd", [E, FF], BF16)
    wuTd = nc.dram_tensor("wuTd", [E, FF], BF16)
    wdTp = nc.dram_tensor("wdTp", [EC, 128, FF], BF16)
    hbuf = nc.dram_tensor("hbuf", [FF, TOWN], BF16)
    aofm = nc.dram_tensor("aofm", [E, TOWN], BF16)

    arA = nc.alloc_sbuf_tensor("arA", [128, 32768], BF16)
    arB = nc.alloc_sbuf_tensor("arB", [128, 32768], BF16)
    idxb2 = nc.alloc_sbuf_tensor("idxb2", [128, 4096], I32)
    dqw = nc.alloc_sbuf_tensor("dqw", [128, 8192], BF16)
    stg = nc.alloc_sbuf_tensor("stg", [128, 8192], BF16)
    rows_f = nc.alloc_sbuf_tensor("rows_f", [1, 2048], F32)
    rows_b = nc.alloc_sbuf_tensor("rows_b", [1, 1024], BF16)
    xres = nc.alloc_sbuf_tensor("xres", [128, 2048], F32)
    outstg = nc.alloc_sbuf_tensor("outstg", [128, 2048], F32)
    exps = stg[:, 0:4096]
    cols = nc.alloc_sbuf_tensor("cols", [128, 8], F32)
    smk = nc.alloc_sbuf_tensor("smk", [128, 4], F32)
    sitab = nc.alloc_sbuf_tensor("sitab", [128, 16], F32)
    cst = nc.alloc_sbuf_tensor("cst", [128, 2], F32)
    onescol = nc.alloc_sbuf_tensor("onescol", [128, 2], BF16)
    onesrow = nc.alloc_sbuf_tensor("onesrow", [1, 128], BF16)
    lutb = nc.alloc_sbuf_tensor("lutb", [128, 16], F32)
    absb = nc.alloc_sbuf_tensor("absb", [128, 1024], BF16)
    eyes = nc.alloc_sbuf_tensor("eyes", [128, 128], BF16)
    ps = nc.alloc_psum_tensor("ps", [128, 4096], F32)

    def bank(i, n=1):
        return ps[:, i * 512:(i + n) * 512]

    stgf = stg.bitcast(F32)

    idxb2_bf = idxb2.bitcast(BF16)

    def ao_chunk(ec):
        if ec < 8:
            return dqw[:, ec * 1024:(ec + 1) * 1024]
        return idxb2_bf[:, (ec - 8) * 1024:(ec - 7) * 1024]

    S = p.sem

    eye_s = nc.inline_tensor(
        (np.eye(128) * (0.125 ** 0.5)).astype(ml_dtypes.bfloat16), name="eye_s")

    p.dma("sy", eyes[:, :], eye_s.ap(), incs=[(S("c_ld"), 16)])
    p.dma("sy", lutb[:, :], lutb_in, incs=[(S("c_ld"), 16)])
    p.wait("ve", S("c_ld"), 32)
    p.emit("ve", lambda v: v.memset(onescol[:, :], 1.0))
    p.emit("ve", lambda v: v.memset(onesrow[:, :], 1.0))
    p.emit("ve", lambda v: v.memset(cst[:, 0:1], -32.0))
    p.emit("ve", lambda v: v.memset(cst[:, 1:2], 0.125), incs=[(S("c_init"), 1)])

    dq_tiles = []

    def idx_tile_aps(src, dst, c0, w, k0, nk):
        i_ap = src.rearrange("(k p) o -> p k o", p=128)[:, k0:k0 + nk, c0:c0 + w]
        o_ap = dst.ap().rearrange("(k p) o -> p k o", p=128)[:, k0:k0 + nk, c0:c0 + w]
        return i_ap, o_ap, nk * w

    for og in range(6):
        c0, w = (og * 1024, 1024) if og < 5 else (5120, 384)
        for srcT, dst in ((gidxT, wgTd), (uidxT, wuTd)):
            for g in range(8):
                dq_tiles.append(idx_tile_aps(srcT, dst, c0, w, g * 2, 2))
    for oc in range(EC):
        for k0, nk in ((0, 16), (16, 16), (32, 11)):
            i_ap = didxT.rearrange("(k p) o -> p k o", p=128)[:, k0:k0 + nk, oc * 128:(oc + 1) * 128]
            o_ap = wdTp.ap()[oc][:, k0 * 128:(k0 + nk) * 128]
            dq_tiles.append((i_ap, o_ap, nk * 128))
    N_DQ = len(dq_tiles)

    pc_k, pc_q, pc_v, pc_o = S("pc_k"), S("pc_q"), S("pc_v"), S("pc_o")
    precasts = [(wkb_d.ap(), wk, pc_k), (wqb_d.ap(), wq, pc_q), (wvb_d.ap(), wv, pc_v)] + [
        (wop_d.ap()[oc].rearrange("p (ecc o) -> p ecc o", o=128),
         wo.rearrange("(ecc p) o -> p ecc o", p=128)[:, :, oc * 128:(oc + 1) * 128], pc_o)
        for oc in range(EC)
    ]
    pc_state = {"emitted": 0}

    def pc_pump():
        if pc_state["emitted"] < len(precasts):
            o_ap, i_ap, sem = precasts[pc_state["emitted"]]
            p.dma("gp", o_ap, i_ap, incs=[(sem, 16)])
            pc_state["emitted"] += 1

    dq_state = {"emitted": 0, "loaded": 0}
    dq_ld = [S("dq_ld0"), S("dq_ld1")]
    dq_cv = S("dq_cv")
    dq_ve = S("dq_ve")
    dq_st = [S("dq_st0"), S("dq_st1")]
    dq_slot_st = [[0], [0]]
    dq_ok = S("dq_ok")
    dq_bounds = {16 * (j + 1): j + 1 for j in range(6)}
    dq_bounds.update({96 + 3 * (oc + 1): 7 + oc for oc in range(EC)})
    dq_nst = {"n": 0}

    dq_micro = {"t": 0, "step": 0}

    def dq_pump(n):
        for _ in range(n):
            t = dq_micro["t"]
            if t >= N_DQ:
                return
            step = dq_micro["step"]
            i_ap, o_dst, ncol = dq_tiles[t]
            buf = t % 2
            ibuf = idxb2[:, buf * 2048: buf * 2048 + ncol]
            acc = dqw[:, (2 + buf) * 2048: (2 + buf) * 2048 + ncol]
            idxbf = dqw[:, 0:ncol]
            plane = dqw[:, 2048:2048 + ncol]
            if step == 0:
                ibuf3 = ibuf.rearrange("p (k o) -> p k o", o=i_ap.shape[2])
                if t >= 2:
                    p.wait("gp", dq_cv, t - 1)
                p.dma("gp", ibuf3, i_ap, incs=[(dq_ld[buf], 16)])
                p.wait("ve", dq_ld[buf], 16 * (t // 2 + 1))
                if t >= 2:
                    p.wait("ve", dq_st[buf], 16 * dq_slot_st[buf][0])
                p.emit("ve", lambda v, a=ibuf, b=idxbf: v.tensor_copy(b, a), incs=[(dq_cv, 1)])
                p.wait("ve", dq_cv, t + 1)
                p.emit("ve", lambda v, b=idxbf, a=acc: v.tensor_scalar(
                    a, b, 0.0, lutb[:, 0:1], OP.is_equal, OP.mult))
                dq_micro["step"] = 1
            elif step <= 15:
                l = step
                p.emit("ve", lambda v, b=idxbf, pl=plane, l=l: v.tensor_scalar(
                    pl, b, float(l), lutb[:, l:l + 1], OP.is_equal, OP.mult))
                inc = [(dq_ve, 1)] if l == 15 else []
                p.emit("ve", lambda v, a=acc, pl=plane: v.tensor_tensor(a, a, pl, OP.add), incs=inc)
                dq_micro["step"] = step + 1
            else:
                p.wait("gp", dq_ve, t + 1)
                if len(o_dst.shape) == 3:
                    accv = acc.rearrange("p (k o) -> p k o", o=o_dst.shape[2])
                else:
                    accv = acc
                p.dma("gp", o_dst, accv, incs=[(dq_st[buf], 16)])
                dq_slot_st[buf][0] += 1
                dq_nst["n"] += 1
                if dq_nst["n"] in dq_bounds:
                    p.wait("gp", dq_st[0], 16 * dq_slot_st[0][0])
                    p.wait("gp", dq_st[1], 16 * dq_slot_st[1][0])
                    p.emit("gp", lambda g: g.nop(), incs=[(dq_ok, 1)])
                if (t + 1) % 2 == 0:
                    pc_pump()
                dq_micro["t"] = t + 1
                dq_micro["step"] = 0
                dq_state["emitted"] = t + 1

    def dq_tiles_pump(n):
        dq_pump(n * 17)

    def dq_ok_gateup(og):
        return og + 1


    a_ld = [S("a_ld0"), S("a_ld1")]
    a_sq = S("a_sq")
    a_s1 = S("a_s1")
    a_s2 = S("a_s2")
    a_stats = S("a_stats")
    a_sd = S("a_sd")
    a_rows = S("a_rows")
    a_bc = S("a_bc")
    a_bcrd = S("a_bcrd")
    a_xn = S("a_xn")
    a_xnst = [S("a_xnst0"), S("a_xnst1")]
    a_tp = S("a_tp")
    a_tpe = S("a_tpe")
    a_tmst = [S("a_tmst0"), S("a_tmst1")]
    b_tld = S("b_tld")
    b_chunk = S("b_chunk")
    b_pe = S("b_pe")
    b_ev = S("b_ev")
    b_st = [S("b_st0"), S("b_st1"), S("b_st2"), S("b_st3")]
    vf = S("vf")

    def vfence(inst_builder):
        p.emit("ve", inst_builder, incs=[(vf, 1)])
        p.wait("ve", vf)

    p.wait("te", S("c_init"))
    n_gh = 0
    n_tp = 0

    arAf = arA.bitcast(F32)
    a_cv = S("a_cv")
    for tt in range(NAT):
        q = tt % 2
        tau = tt * ATILE
        xf32h = arAf[:, 0:4096]
        xbf = arA[:, 8192 + q * 8192: 16384 + q * 8192]
        xbf3 = xbf.rearrange("p (k t) -> p k t", t=ATILE)
        sq = arB[:, 0:8192]
        sq3 = sq.rearrange("p (k t) -> p k t", t=ATILE)
        xn = arB[:, 8192:16384] if q == 0 else arA[:, 24576:32768]
        xn3 = xn.rearrange("p (k t) -> p k t", t=ATILE)
        ps_s1 = ps[0:1, 0:512]
        ps_s2 = ps[0:1, 512:1024]
        ps_rb = bank(2)
        ps_mrb = bank(3)
        ps_tp = ps[:, 1024:2048].bitcast(BF16)
        m_row = rows_f[0:1, 0:512]
        v_row = rows_f[0:1, 512:1024]
        sd_row = rows_f[0:1, 1024:1536]
        mr_row = rows_f[0:1, 1536:2048]
        tmp_row = rows_f[0:1, 512:1024]
        rb_row = rows_b[0:1, 0:512]
        mrb_row = rows_b[0:1, 512:1024]
        rbsb = absb[:, 0:512]
        mrbsb = absb[:, 512:1024]

        for hh in range(2):
            if tt + hh > 0:
                p.wait("sy", a_cv, 2 * tt + hh)
            p.dma("sy", xf32h.rearrange("p (k t) -> p k t", t=ATILE),
                  xT.rearrange("(k p) t -> p k t", p=128)[:, hh * 8:(hh + 1) * 8, tau:tau + ATILE],
                  incs=[(a_ld[0], 16)])
            p.wait("sc", a_ld[0], 16 * (2 * tt + hh + 1))
            if hh == 0 and tt >= 2:
                p.wait("sc", a_xn, tt - 1)
            p.emit("sc", lambda s, a=xf32h, b=xbf[:, hh * 4096:(hh + 1) * 4096]:
                   s.copy(b, a), incs=[(a_cv, 1)])
        if tt >= 1:
            p.wait("sc", a_s2, tt)
        p.emit("sc", lambda s, a=xbf, b=sq: s.activation(b, a, AF.Square), incs=[(a_sq, 1)])

        p.wait("te", a_cv, 2 * tt + 2)
        if tt >= 1:
            p.wait("te", a_stats, tt)
        for k in range(EC):
            p.emit("te", lambda t, o=ps_s1, r=xbf3, k=k: t.matmul(
                o, onescol[:, 0:1], r[:, k, :], start=(k == 0), stop=(k == EC - 1)),
                incs=[(a_s1, 1)] if k == EC - 1 else [])
        p.wait("te", a_sq, tt + 1)
        for k in range(EC):
            p.emit("te", lambda t, o=ps_s2, r=sq3, k=k: t.matmul(
                o, onescol[:, 0:1], r[:, k, :], start=(k == 0), stop=(k == EC - 1)),
                incs=[(a_s2, 1)] if k == EC - 1 else [])

        p.wait("ve", a_s1, tt + 1)
        p.wait("ve", a_s2, tt + 1)
        vfence(lambda v, a=ps_s1, o=m_row: v.tensor_scalar(o, a, 1.0 / E, None, OP.mult))
        p.emit("ve", lambda v, a=ps_s2, o=v_row: v.tensor_scalar(o, a, 1.0 / E, 1e-5, OP.mult, OP.add),
               incs=[(a_stats, 1)])
        p.wait("ve", a_stats)
        vfence(lambda v, o=sd_row, a=m_row: v.tensor_tensor(o, a, a, OP.mult))
        p.emit("ve", lambda v, o=v_row, a=sd_row: v.tensor_tensor(o, o, a, OP.subtract),
               incs=[(S("a_var"), 1)])
        p.wait("sc", S("a_var"), tt + 1)
        p.emit("sc", lambda s, o=sd_row, a=v_row: s.activation(o, a, AF.Sqrt),
               incs=[(a_sd, 1)])
        p.wait("ve", a_sd, tt + 1)
        vfence(lambda v, o=tmp_row, a=sd_row: v.reciprocal(o, a))
        vfence(lambda v, o=mr_row, a=m_row, b=tmp_row: v.tensor_tensor(o, a, b, OP.mult))
        vfence(lambda v, o=rb_row, a=tmp_row: v.tensor_copy(o, a))
        p.emit("ve", lambda v, o=mrb_row, a=mr_row: v.tensor_copy(o, a),
               incs=[(a_rows, 1)])
        p.wait("ve", a_rows)

        p.wait("te", a_rows, tt + 1)
        if tt >= 1:
            p.wait("te", a_tpe, n_tp)
        p.emit("te", lambda t, o=ps_rb, r=rb_row: t.matmul(o, onesrow[0:1, :], r, start=True, stop=True))
        p.emit("te", lambda t, o=ps_mrb, r=mrb_row: t.matmul(o, onesrow[0:1, :], r, start=True, stop=True),
               incs=[(a_bc, 1)])

        p.wait("ve", a_bc, tt + 1)
        vfence(lambda v, o=rbsb, a=ps_rb: v.tensor_copy(o, a))
        p.emit("ve", lambda v, o=mrbsb, a=ps_mrb: v.tensor_copy(o, a),
               incs=[(a_bcrd, 1)])
        p.wait("ve", a_bcrd)
        if tt >= 2:
            p.wait("ve", a_tp, 4 * (tt - 1))
            p.wait("ve", a_xnst[q], 16 * ((tt - 2) // 2 + 1))
        vfence(lambda v, o=xn3, a=xbf3, b=rbsb: v.tensor_tensor(
            o, a, bc16(b, EC, ATILE), OP.mult))
        p.emit("ve", lambda v, o=xn3, a=xn3, b=mrbsb:
               v.tensor_tensor(o, a, bc16(b, EC, ATILE), OP.subtract),
               incs=[(a_xn, 1)])

        p.wait("sy", a_xn, tt + 1)
        p.dma("sy", xnfm.ap().rearrange("(k p) t -> p k t", p=128)[:, :, tau:tau + ATILE],
              xn3, incs=[(a_xnst[q], 16)])

        for ts4 in range(4):
            p.wait("te", a_bcrd, tt + 1)
            if n_tp >= 1:
                p.wait("te", a_tpe, n_tp)
            p.wait("te", a_xn, tt + 1)
            for ecc in range(EC):
                p.emit("te", lambda t, o=ps_tp[:, ecc * 128:(ecc + 1) * 128],
                       i=xn3[:, ecc, ts4 * 128:(ts4 + 1) * 128]:
                       t.transpose(o, i, eyes[:, :]),
                       incs=[(a_tp, 1)] if ecc == EC - 1 else [])
            p.wait("ve", a_tp, n_tp + 1)
            tpslot = outstg.bitcast(BF16)[:, (n_tp % 2) * 2048:(n_tp % 2) * 2048 + 2048]
            if n_tp >= 2:
                p.wait("ve", a_tmst[n_tp % 2], 16 * ((n_tp - 2) // 2 + 1))
            p.emit("ve", lambda v, o=tpslot, a=ps_tp: v.tensor_scalar(
                o, a, 0.125 ** 0.5, None, OP.mult), incs=[(a_tpe, 1)])
            p.wait("sy", a_tpe, n_tp + 1)
            p.dma("sy", xntm_d.ap()[tau + ts4 * 128: tau + (ts4 + 1) * 128, :],
                  tpslot, incs=[(a_tmst[n_tp % 2], 16)])
            n_tp += 1

        dq_pump(6)

        if tt % 2 == 1:
            c = tt // 2
            btile = arB[:, 16384:32768]
            bt3 = btile.rearrange("p (n e) -> p n e", e=2048)
            p.wait("sy", a_tmst[0], 16 * 2 * (tt + 1))
            p.wait("sy", a_tmst[1], 16 * 2 * (tt + 1))
            if c >= 1:
                p.wait("sy", b_chunk, c)
            p.dma("sy", bt3, xntm_d.ap().rearrange("(n p) e -> p n e", p=128)[:, 8 * c:8 * c + 8, :],
                  incs=[(b_tld, 16)])
            p.wait("te", b_tld, 16 * (c + 1))
            for eg in range(EC):
                for half in range(2):
                    slot = 4 + 2 * (n_gh % 2)
                    if n_gh >= 2:
                        p.wait("te", b_ev, n_gh - 1)
                    for ts in range(8):
                        lhsT = bt3[:, ts, eg * 128:(eg + 1) * 128]
                        for j in range(2):
                            out = ps[:, slot * 512 + j * 512: slot * 512 + (j + 1) * 512]
                            rhs = bt3[:, ts, half * 1024 + j * 512: half * 1024 + (j + 1) * 512]
                            p.emit("te", lambda t, o=out, l=lhsT, r=rhs, s=(ts == 0), e=(ts == 7):
                                   t.matmul(o, l, r, start=s, stop=e),
                                   incs=[(b_pe, 1)] if (ts == 7 and j == 1) else [])
                    p.wait("sc", b_pe, n_gh + 1)
                    sslot = stgf[:, (n_gh % 4) * 1024:(n_gh % 4) * 1024 + 1024]
                    if n_gh >= 4:
                        p.wait("sc", b_st[n_gh % 4], 16 * ((n_gh - 4) // 4 + 1))
                    p.emit("sc", lambda s, o=sslot, a=ps[:, slot * 512: slot * 512 + 1024]:
                           s.copy(o, a), incs=[(b_ev, 1)])
                    p.wait("gp", b_ev, n_gh + 1)
                    kw = {} if c == 0 else {"accum_op": OP.add}
                    p.dma("gp", gdram.ap().rearrange("(k p) t -> p k t", p=128)[
                        :, eg:eg + 1, half * 1024:(half + 1) * 1024],
                        sslot.rearrange("p (o t) -> p o t", o=1),
                        incs=[(b_st[n_gh % 4], 16)], **kw)
                    n_gh += 1
                dq_pump(4)
            p.emit("te", lambda t: t.nop(), incs=[(b_chunk, 1)])

    c_wk = S("c_wk")
    c_gk = S("c_gk")
    c_pe = S("c_pe")
    c_ev = S("c_ev")
    c_st = [S("c_st0"), S("c_st1")]

    for bs in b_st:
        p.wait("gp", bs, 16 * 64)
    for kc in range(EC):
        p.dma("gp", arB[:, kc * 2048:(kc + 1) * 2048].rearrange("p (o t) -> p o t", o=1),
              gdram.ap().rearrange("(k p) t -> p k t", p=128)[:, kc:kc + 1, :], incs=[(c_gk, 16)])
    p.wait("sy", b_chunk, NBC)
    p.wait("sy", a_xnst[0], 16 * 8)
    p.wait("sy", a_xnst[1], 16 * 8)
    p.wait("sy", pc_k, 16)
    p.dma("sy", arA.rearrange("p (k t) -> p k t", t=2048),
          wkb_d.ap().rearrange("(k p) t -> p k t", p=128), incs=[(c_wk, 16)])

    for ec in range(EC):
        q = ec % 2
        pbank = 0 if q == 0 else 4
        p.wait("te", c_wk, 16)
        p.wait("te", c_gk, 16 * EC)
        if ec >= 2:
            p.wait("te", c_ev, ec - 1)
        for fc in range(EC):
            lhsT = arB[:, fc * 2048 + ec * 128: fc * 2048 + (ec + 1) * 128]
            for j in range(4):
                rhs = arA[:, fc * 2048 + j * 512: fc * 2048 + (j + 1) * 512]
                out = ps[:, pbank * 512 + j * 512: pbank * 512 + (j + 1) * 512]
                p.emit("te", lambda t, o=out, l=lhsT, r=rhs, s=(fc == 0), e=(fc == EC - 1):
                       t.matmul(o, l, r, start=s, stop=e),
                       incs=[(c_pe, 1)] if (fc == EC - 1 and j == 3) else [])
        p.wait("sc", c_pe, ec + 1)
        n1stg = stg[:, (ec % 2) * 2048:(ec % 2) * 2048 + 2048]
        if ec >= 2:
            p.wait("sc", c_st[ec % 2], 16 * ((ec - 2) // 2 + 1))
        p.emit("sc", lambda s, o=n1stg, a=ps[:, pbank * 512: pbank * 512 + 2048]:
               s.copy(o, a), incs=[(c_ev, 1)])
        p.wait("sy", c_ev, ec + 1)
        p.dma("sy", n1d.ap().rearrange("(k p) t -> p k t", p=128)[:, ec:ec + 1, :],
              n1stg.rearrange("p (o t) -> p o t", o=1), incs=[(c_st[ec % 2], 16)])
        dq_pump(10)

    s_wq = S("s_wq")
    s_n1 = S("s_n1")
    s_pe = S("s_pe")
    s_nm = S("s_nm")
    s_cp = S("s_cp")
    s_exp = S("s_exp")
    s_wt = S("s_wt")
    s_si = S("s_si")
    s_wtst = [S("s_wtst0"), S("s_wtst1")]

    p.wait("sy", c_pe, EC)
    p.wait("sy", pc_q, 16)
    p.dma("sy", arA.rearrange("p (k t) -> p k t", t=2048),
          wqb_d.ap().rearrange("(k p) t -> p k t", p=128), incs=[(s_wq, 16)])
    for kc in range(EC):
        p.wait("sy", c_pe, EC)
        p.wait("sy", c_st[0], 16 * 8)
        p.wait("sy", c_st[1], 16 * 8)
        p.dma("sy", arB[:, kc * 2048:(kc + 1) * 2048].rearrange("p (o t) -> p o t", o=1),
              n1d.ap().rearrange("(k p) t -> p k t", p=128)[:, kc:kc + 1, :], incs=[(s_n1, 16)])

    for ic in range(EC):
        q = ic % 2
        pbank = 0 if q == 0 else 4
        p.wait("te", s_wq, 16)
        p.wait("te", s_n1, 16 * EC)
        if ic >= 2:
            p.wait("te", s_cp, ic - 1)
        for ecc in range(EC):
            lhsT = arA[:, ecc * 2048 + ic * 128: ecc * 2048 + (ic + 1) * 128]
            for j in range(4):
                rhs = arB[:, ecc * 2048 + j * 512: ecc * 2048 + (j + 1) * 512]
                out = ps[:, pbank * 512 + j * 512: pbank * 512 + (j + 1) * 512]
                p.emit("te", lambda t, o=out, l=lhsT, r=rhs, s=(ecc == 0), e=(ecc == EC - 1):
                       t.matmul(o, l, r, start=s, stop=e),
                       incs=[(s_pe, 1)] if (ecc == EC - 1 and j == 3) else [])
        psc = ps[:, pbank * 512: pbank * 512 + 2048]
        sslot = (outstg if ic % 2 == 0 else xres)[:, 0:2048]
        cm = cols[:, q * 4 + 0: q * 4 + 1]
        nm = cols[:, q * 4 + 1: q * 4 + 2]
        sm = smk[:, ic % 4: ic % 4 + 1]
        si = sitab[:, ic: ic + 1]
        ex = exps[:, q * 2048:(q + 1) * 2048]
        p.wait("sc", s_pe, ic + 1)
        if ic >= 2:
            p.wait("sc", s_exp, ic - 1)
        p.emit("sc", lambda s, o=sslot, a=psc, c=cm: s.activation(o, a, AF.Copy, accum_out=c),
               incs=[(s_cp, 1)])
        p.wait("gp", s_cp, ic + 1)
        p.emit("gp", lambda g, o=nm, a=cm: g.tensor_scalar(
            o, a, -1.0 / 2048.0, -95.0, OP.mult, OP.add), incs=[(s_nm, 1)])
        p.wait("sc", s_nm, ic + 1)
        if ic >= 2:
            p.wait("sc", s_wtst[ic % 2], 16 * ((ic - 2) // 2 + 1))
        if ic >= 4:
            p.wait("sc", s_si, ic - 3)
        p.emit("sc", lambda s, o=ex, a=sslot, b=nm, c=sm: s.activation(
            o, a, AF.Exp, bias=b, accum_out=c), incs=[(s_exp, 1)])
        p.wait("gp", s_exp, ic + 1)
        p.dma("gp", wtd.ap().rearrange("(k p) t -> p k t", p=128)[:, ic:ic + 1, :],
              ex.rearrange("p (o t) -> p o t", o=1), incs=[(s_wtst[ic % 2], 16)])
        p.wait("ve", s_exp, ic + 1)
        p.emit("ve", lambda v, o=si, a=sm: v.reciprocal(o, a), incs=[(s_si, 1)])
        dq_pump(8)

    v_wv = S("v_wv")
    v_xld = [S("v_xld0"), S("v_xld1")]
    v_pe = S("v_pe")
    v_ev = S("v_ev")
    v_st = S("v_st")

    p.wait("sy", s_pe, EC)
    p.wait("sy", pc_v, 16)
    p.dma("sy", arA.rearrange("p (k t) -> p k t", t=2048),
          wvb_d.ap().rearrange("(k p) t -> p k t", p=128), incs=[(v_wv, 16)])

    VT = 1024
    for vt in range(TOWN // VT):
        q = vt % 2
        xnt = arB[:, q * 16384:(q + 1) * 16384]
        xnt3 = xnt.rearrange("p (k t) -> p k t", t=VT)
        p.wait("sy", s_pe, EC)
        if vt >= 2:
            p.wait("sy", v_pe, (vt - 1) * EC)
        p.dma("sy", xnt3,
              xnfm.ap().rearrange("(k p) t -> p k t", p=128)[:, :, OWN0 + vt * VT: OWN0 + (vt + 1) * VT],
              incs=[(v_xld[q], 16)])
        p.wait("te", v_wv, 16)
        p.wait("te", v_xld[q], 16 * (vt // 2 + 1))
        for fc in range(EC):
            pbank = (fc % 4) * 2
            if vt * EC + fc >= 4:
                p.wait("te", v_ev, vt * EC + fc - 3)
            for ecc in range(EC):
                lhsT = arA[:, ecc * 2048 + fc * 128: ecc * 2048 + (fc + 1) * 128]
                for j in range(2):
                    out = ps[:, pbank * 512 + j * 512: pbank * 512 + (j + 1) * 512]
                    p.emit("te", lambda t, o=out, l=lhsT, r=xnt3[:, ecc, j * 512:(j + 1) * 512],
                           s=(ecc == 0), e=(ecc == EC - 1): t.matmul(o, l, r, start=s, stop=e),
                           incs=[(v_pe, 1)] if (ecc == EC - 1 and j == 1) else [])
            p.wait("sc", v_pe, vt * EC + fc + 1)
            slot = stg[:, (fc % 8) * 1024:(fc % 8 + 1) * 1024]
            if vt == 0 and fc < 4:
                p.wait("sc", s_wtst[0], 16 * 8)
                p.wait("sc", s_wtst[1], 16 * 8)
            if vt * EC + fc >= 8:
                p.wait("sc", v_st, 16 * ((vt * EC + fc - 7 + 7) // 8))
            p.emit("sc", lambda s, o=slot, a=ps[:, pbank * 512: pbank * 512 + 1024]:
                   s.copy(o, a), incs=[(v_ev, 1)])
            if fc % 8 == 7:
                p.wait("sc", v_ev, vt * EC + fc + 1)
                h8 = fc // 8
                p.dma("sc", vfm.ap().rearrange("(g p) t -> p g t", p=128)[:, h8 * 8:(h8 + 1) * 8, vt * VT:(vt + 1) * VT],
                      stg[:, 0:8192].rearrange("p (g t) -> p g t", t=VT), incs=[(v_st, 16)])
            dq_pump(4)

    o_wt = S("o_wt")
    o_vld = [S("o_vld0"), S("o_vld1")]
    o_pe = S("o_pe")
    o_ev = S("o_ev")
    o_st = S("o_st")

    for fc in range(EC):
        p.wait("sy", v_pe, (TOWN // VT) * EC)
        p.wait("sy", s_wtst[0], 16 * 8)
        p.wait("sy", s_wtst[1], 16 * 8)
        p.dmat("sy", arA[:, fc * 2048:(fc + 1) * 2048], wtd.ap()[:, fc * 128:(fc + 1) * 128],
               incs=[(o_wt, 16)])

    for vt in range(TOWN // VT):
        q = vt % 2
        vtile = arB[:, q * 16384:(q + 1) * 16384]
        vtile3 = vtile.rearrange("p (k t) -> p k t", t=VT)
        p.wait("sy", v_st, 16 * 2 * (TOWN // VT))
        if vt >= 2:
            p.wait("sy", o_pe, (vt - 1) * EC)
        p.dma("sy", vtile3,
              vfm.ap().rearrange("(k p) t -> p k t", p=128)[:, :, vt * VT:(vt + 1) * VT],
              incs=[(o_vld[q], 16)])
        p.wait("te", o_wt, 16 * EC)
        p.wait("te", o_vld[q], 16 * (vt // 2 + 1))
        for ecc in range(EC):
            pbank = (ecc % 4) * 2
            if vt * EC + ecc >= 4:
                p.wait("te", o_ev, vt * EC + ecc - 3)
            for fc in range(EC):
                lhsT = arA[:, fc * 2048 + ecc * 128: fc * 2048 + (ecc + 1) * 128]
                for j in range(2):
                    out = ps[:, pbank * 512 + j * 512: pbank * 512 + (j + 1) * 512]
                    p.emit("te", lambda t, o=out, l=lhsT, r=vtile3[:, fc, j * 512:(j + 1) * 512],
                           s=(fc == 0), e=(fc == EC - 1): t.matmul(o, l, r, start=s, stop=e),
                           incs=[(o_pe, 1)] if (fc == EC - 1 and j == 1) else [])
            p.wait("ve", o_pe, vt * EC + ecc + 1)
            slot = stg[:, (ecc % 8) * 1024:(ecc % 8 + 1) * 1024]
            if vt * EC + ecc >= 8:
                p.wait("ve", o_st, 16 * ((vt * EC + ecc - 7 + 7) // 8))
            p.emit("ve", lambda v, o=slot, a=ps[:, pbank * 512: pbank * 512 + 1024],
                   b=sitab[:, ecc:ecc + 1]: v.tensor_scalar(o, a, b, None, OP.mult),
                   incs=[(o_ev, 1)])
            if ecc % 8 == 7:
                p.wait("sc", o_ev, vt * EC + ecc + 1)
                h8 = ecc // 8
                p.dma("sc", aofm.ap().rearrange("(g p) t -> p g t", p=128)[:, h8 * 8:(h8 + 1) * 8, vt * VT:(vt + 1) * VT],
                      stg[:, 0:8192].rearrange("p (g t) -> p g t", t=VT), incs=[(o_st, 16)])
            dq_pump(3)

    g_xld = S("g_xld")
    g_pang = [S("g_pang0"), S("g_pang1")]
    g_panu = [S("g_panu0"), S("g_panu1")]
    g_gate = S("g_gate")
    g_si = S("g_si")
    g_up = S("g_up")
    g_h = S("g_h")
    g_hst = S("g_hst")
    g_sg = S("g_sg")

    GUT = 2048
    p.wait("sc", o_st, 16 * 2 * (TOWN // VT))
    p.wait("ve", o_st, 16 * 2 * (TOWN // VT))
    subgroups = [(i * 4, 4) for i in range(10)] + [(40, 3)]
    pan_w = 512

    npan = 0
    n_o = 0
    for tb in range(TOWN // GUT):
        xnt3 = arA.rearrange("p (k t) -> p k t", t=GUT)
        if tb == 0:
            p.wait("sy", o_pe, (TOWN // VT) * EC)
        else:
            p.wait("sy", g_sg, len(subgroups) * tb)
        p.dma("sy", xnt3,
              xnfm.ap().rearrange("(k p) t -> p k t", p=128)[:, :, OWN0 + tb * GUT: OWN0 + (tb + 1) * GUT],
              incs=[(g_xld, 16)])
        for sgi, (o0, no) in enumerate(subgroups):
            sgq = sgi % 2
            wcols = no * 128
            gp_ap = arB[:, sgq * 16384: sgq * 16384 + 16 * pan_w]
            up_ap = arB[:, sgq * 16384 + 16 * pan_w: sgq * 16384 + 32 * pan_w]
            og_hi = ((o0 + no) * 128 - 1) // 1024
            glob_sg = tb * len(subgroups) + sgi
            for w_dram, pan, psem in ((wgTd, gp_ap, g_pang[sgq]), (wuTd, up_ap, g_panu[sgq])):
                p.wait("sy", dq_ok, dq_ok_gateup(og_hi))
                if glob_sg >= 2:
                    p.wait("sy", g_sg, glob_sg - 1)
                p.dma("sy", pan.rearrange("p (k w) -> p k w", w=pan_w)[:, :, 0:wcols],
                      w_dram.ap().rearrange("(k p) o -> p k o", p=128)[:, :, o0 * 128:(o0 + no) * 128],
                      incs=[(psem, 16)])
            p.wait("te", g_pang[sgq])
            p.wait("te", g_panu[sgq])
            p.wait("te", g_xld, 16 * (tb + 1))
            for oi in range(no):
                for th in range(2):
                    gslot = (n_o % 2) * 2
                    uslot = 4 + (n_o % 2) * 2
                    if n_o >= 2:
                        p.wait("te", g_si, n_o - 1)
                    for hc in range(EC):
                        lhsT = gp_ap[:, hc * pan_w + oi * 128: hc * pan_w + (oi + 1) * 128]
                        for j in range(2):
                            out = ps[:, gslot * 512 + j * 512: gslot * 512 + (j + 1) * 512]
                            p.emit("te", lambda t, o=out, l=lhsT,
                                   r=xnt3[:, hc, th * 1024 + j * 512: th * 1024 + (j + 1) * 512],
                                   s=(hc == 0), e=(hc == EC - 1): t.matmul(o, l, r, start=s, stop=e),
                                   incs=[(g_gate, 1)] if (hc == EC - 1 and j == 1) else [])
                    p.wait("sc", g_gate, n_o + 1)
                    silu_sb = stg[:, (n_o % 2) * 1024: (n_o % 2) * 1024 + 1024]
                    if n_o >= 2:
                        p.wait("sc", g_h, n_o - 1)
                    p.emit("sc", lambda s, o=silu_sb, a=ps[:, gslot * 512: gslot * 512 + 1024]:
                           s.activation(o, a, AF.Silu), incs=[(g_si, 1)])
                    if n_o >= 2:
                        p.wait("te", g_h, n_o - 1)
                    for hc in range(EC):
                        lhsT = up_ap[:, hc * pan_w + oi * 128: hc * pan_w + (oi + 1) * 128]
                        for j in range(2):
                            out = ps[:, uslot * 512 + j * 512: uslot * 512 + (j + 1) * 512]
                            p.emit("te", lambda t, o=out, l=lhsT,
                                   r=xnt3[:, hc, th * 1024 + j * 512: th * 1024 + (j + 1) * 512],
                                   s=(hc == 0), e=(hc == EC - 1): t.matmul(o, l, r, start=s, stop=e),
                                   incs=[(g_up, 1)] if (hc == EC - 1 and j == 1) else [])
                    p.wait("ve", g_up, n_o + 1)
                    p.wait("ve", g_si, n_o + 1)
                    h_sb = stg[:, 2048 + (n_o % 2) * 1024: 2048 + (n_o % 2) * 1024 + 1024]
                    if n_o >= 2:
                        p.wait("ve", g_hst, 16 * (n_o - 1))
                    p.emit("ve", lambda v, o=h_sb, a=silu_sb, b=ps[:, uslot * 512: uslot * 512 + 1024]:
                           v.tensor_tensor(o, a, b, OP.mult), incs=[(g_h, 1)])
                    p.wait("sc", g_h, n_o + 1)
                    o_glob = o0 + oi
                    p.dma("sc", hbuf.ap()[o_glob * 128:(o_glob + 1) * 128,
                                          tb * GUT + th * 1024: tb * GUT + (th + 1) * 1024],
                          h_sb, incs=[(g_hst, 16)])
                    n_o += 1
                dq_pump(15)
            p.emit("te", lambda t: t.nop(), incs=[(g_sg, 1)])

    dq_pump(10 ** 6)

    f_h = S("f_h")
    f_ao = S("f_ao")
    f_wd = [S("f_wd0"), S("f_wd1"), S("f_wd2")]
    f_wo = [S("f_wo0"), S("f_wo1"), S("f_wo2")]
    f_x = [S("f_x0"), S("f_x1")]
    f_pe = S("f_pe")
    f_ev = S("f_ev")
    f_out = [S("f_out0"), S("f_out1")]
    f_tb = S("f_tb")

    FT = 1024
    nf = 0
    for tb in range(TOWN // FT):
        q = tb % 2
        p.wait("sy", g_hst, 16 * 172)
        if tb >= 1:
            p.wait("sy", f_tb, tb)
        p.dma("sy", arA.rearrange("p (k t) -> p k t", t=FT),
              hbuf.ap().rearrange("(k p) t -> p k t", p=128)[:, 0:32, tb * FT:(tb + 1) * FT],
              incs=[(f_h, 16)])
        p.dma("sy", arB[:, 0:9216].rearrange("p (k t) -> p k t", t=FT),
              hbuf.ap().rearrange("(k p) t -> p k t", p=128)[:, 32:41, tb * FT:(tb + 1) * FT],
              incs=[(f_h, 16)])
        p.dma("sy", stg[:, 0:2048].rearrange("p (k t) -> p k t", t=FT),
              hbuf.ap().rearrange("(k p) t -> p k t", p=128)[:, 41:43, tb * FT:(tb + 1) * FT],
              incs=[(f_h, 16)])
        p.wait("gp", dq_ok, 22)
        p.wait("gp", o_st, 16 * 2 * (TOWN // VT))
        p.wait("gp", g_hst, 16 * 172)
        p.wait("gp", pc_o, 16 * 16)
        if tb >= 1:
            p.wait("gp", f_tb, tb)
        p.dma("gp", dqw.rearrange("p (k t) -> p k t", t=1024),
              aofm.ap().rearrange("(k p) t -> p k t", p=128)[:, 0:8, tb * FT:(tb + 1) * FT],
              incs=[(f_ao, 16)])
        p.dma("gp", idxb2_bf[:, 0:8192].rearrange("p (k t) -> p k t", t=1024),
              aofm.ap().rearrange("(k p) t -> p k t", p=128)[:, 8:16, tb * FT:(tb + 1) * FT],
              incs=[(f_ao, 16)])

        def h_chunk(hc, tbq=0):
            if hc < 32:
                return arA[:, hc * FT:(hc + 1) * FT]
            if hc < 41:
                return arB[:, (hc - 32) * FT:(hc - 31) * FT]
            return stg[:, (hc - 41) * FT:(hc - 40) * FT]

        for oc in range(EC):
            oq = nf % 2
            o3 = nf % 3
            wd_pan = arB[:, 9216 + o3 * 5504: 9216 + o3 * 5504 + 5504]
            wo_pan = arB[:, 25728 + o3 * 2048: 25728 + o3 * 2048 + 2048]
            p.wait("sy", dq_ok, 7 + oc)
            if nf >= 3:
                p.wait("sy", f_pe, nf - 2)
            p.dma("sy", wd_pan, wdTp.ap()[oc], incs=[(f_wd[o3], 16)])
            p.wait("gp", f_ao, 16 * 2 * (tb + 1))
            if nf >= 3:
                p.wait("gp", f_pe, nf - 2)
            p.dma("gp", wo_pan, wop_d.ap()[oc], incs=[(f_wo[o3], 16)])
            xr = xres[:, oq * 1024: oq * 1024 + 1024]
            if nf >= 2:
                p.wait("sc", f_ev, nf - 1)
            p.dma("sc", xr.rearrange("p (o t) -> p o t", o=1),
                  xT.rearrange("(k p) t -> p k t", p=128)[:, oc:oc + 1, OWN0 + tb * FT: OWN0 + (tb + 1) * FT],
                  incs=[(f_x[oq], 16)])
            pbank = (nf % 2) * 2
            if nf < 2:
                p.wait("te", g_si, 172)
                p.wait("te", g_h, 172)
            p.wait("te", f_wd[o3], 16 * (nf // 3 + 1))
            p.wait("te", f_wo[o3], 16 * (nf // 3 + 1))
            p.wait("te", f_h, 16 * 3 * (tb + 1))
            p.wait("te", f_ao, 16 * 2 * (tb + 1))
            if nf >= 2:
                p.wait("te", f_ev, nf - 1)
            nmm = 0
            for hc in range(FFC):
                lhsT = wd_pan[:, hc * 128:(hc + 1) * 128]
                rhs_t = h_chunk(hc)
                for j in range(2):
                    out = ps[:, pbank * 512 + j * 512: pbank * 512 + (j + 1) * 512]
                    p.emit("te", lambda t, o=out, l=lhsT, r=rhs_t[:, j * 512:(j + 1) * 512],
                           s=(hc == 0): t.matmul(o, l, r, start=s, stop=False))
                    nmm += 1
            for ecc in range(EC):
                lhsT = wo_pan[:, ecc * 128:(ecc + 1) * 128]
                rhs_t = ao_chunk(ecc)
                for j in range(2):
                    out = ps[:, pbank * 512 + j * 512: pbank * 512 + (j + 1) * 512]
                    last = (ecc == EC - 1)
                    p.emit("te", lambda t, o=out, l=lhsT, r=rhs_t[:, j * 512:(j + 1) * 512],
                           e=last: t.matmul(o, l, r, start=False, stop=e),
                           incs=[(f_pe, 1)] if (last and j == 1) else [])
            p.wait("ve", f_pe, nf + 1)
            p.wait("ve", f_x[oq], 16 * (nf // 2 + 1))
            ostg = outstg[:, oq * 1024: oq * 1024 + 1024]
            if nf >= 2:
                p.wait("ve", f_out[oq], 16 * ((nf - 2) // 2 + 1))
            p.emit("ve", lambda v, o=ostg, a=ps[:, pbank * 512: pbank * 512 + 1024], b=xr:
                   v.tensor_tensor(o, a, b, OP.add), incs=[(f_ev, 1)])
            p.wait("sc", f_ev, nf + 1)
            p.dma("sc", outT.rearrange("(k p) t -> p k t", p=128)[:, oc:oc + 1, tb * FT:(tb + 1) * FT],
                  ostg.rearrange("p (o t) -> p o t", o=1), incs=[(f_out[oq], 16)])
            nf += 1
        p.emit("te", lambda t: t.nop(), incs=[(f_tb, 1)])

    assert dq_state["emitted"] == N_DQ, f"dequant tiles not all emitted: {dq_state['emitted']}/{N_DQ}"

    if debug:
        dbg = {}
        p.wait("gp", f_out[0], 16 * 32)
        p.wait("gp", f_out[1], 16 * 32)
        for name, t, shape in (("d_xnfm", xnfm, [E, TFULL]), ("d_n1", n1d, [E, E]),
                               ("d_wt", wtd, [E, E]), ("d_vfm", vfm, [E, TOWN]),
                               ("d_h", hbuf, [FF, TOWN]), ("d_ao", aofm, [E, TOWN]),
                               ("d_wg", wgTd, [E, FF]), ("d_wd", wdTp, [EC, 128, FF])):
            o = nc.dram_tensor(name, shape, BF16, kind="ExternalOutput").ap()
            p.dma("gp", o, t.ap(), incs=[(S("dbg_st"), 16)])
        p.wait("gp", S("dbg_st"), S("dbg_st").c)

    p.wait("gp", f_out[0], 16 * 32)
    p.wait("gp", f_out[1], 16 * 32)

    with nc.Block() as block:
        @block.sync
        def _(eng):
            for f in p.streams["sy"]:
                f(eng)

        @block.gpsimd
        def _(eng):
            for f in p.streams["gp"]:
                f(eng)

        @block.vector
        def _(eng):
            for f in p.streams["ve"]:
                f(eng)

        @block.scalar
        def _(eng):
            for f in p.streams["sc"]:
                f(eng)

        @block.tensor
        def _(eng):
            for f in p.streams["te"]:
                f(eng)

    return nc, p


_cache = {}


def kernel(L0_attn_x, L0_attn_Wq, L0_attn_Wk, L0_attn_Wv, L0_attn_Wo,
           L0_ffn_lut, L0_ffn_gate_idx, L0_ffn_up_idx, L0_ffn_down_idx):
    x = np.asarray(L0_attn_x)
    B, S, Emod = x.shape
    assert (B, S, Emod) == (4, 8192, 2048)
    lut = np.asarray(L0_ffn_lut).astype(np.float32).reshape(16)
    lutb = np.ascontiguousarray(np.broadcast_to(lut, (128, 16))).astype(np.float32)
    gidxT = np.ascontiguousarray(np.asarray(L0_ffn_gate_idx).T).astype(np.int32)
    uidxT = np.ascontiguousarray(np.asarray(L0_ffn_up_idx).T).astype(np.int32)
    didxT = np.ascontiguousarray(np.asarray(L0_ffn_down_idx).T).astype(np.int32)
    wqa = np.ascontiguousarray(np.asarray(L0_attn_Wq)).astype(np.float32)
    wka = np.ascontiguousarray(np.asarray(L0_attn_Wk)).astype(np.float32)
    wva = np.ascontiguousarray(np.asarray(L0_attn_Wv)).astype(np.float32)
    woa = np.ascontiguousarray(np.asarray(L0_attn_Wo)).astype(np.float32)

    if "nc" not in _cache:
        _cache["nc"] = build_program()[0]
    nc = _cache["nc"]

    in_maps = []
    for c in range(8):
        b, h = c // 2, c % 2
        xb = x[b].T.astype(np.float32)
        if h == 1:
            xb = np.concatenate([xb[:, 4096:], xb[:, :4096]], axis=1)
        in_maps.append(dict(
            xT=np.ascontiguousarray(xb), wq=wqa, wk=wka, wv=wva, wo=woa,
            lutb_i=lutb, gidxT=gidxT, uidxT=uidxT, didxT=didxT,
        ))
    res = run_bass_kernel_spmd(nc, in_maps, core_ids=list(range(8)))
    out = np.empty((4, 8192, 2048), dtype=np.float32)
    for c in range(8):
        b, h = c // 2, c % 2
        out[b, h * 4096:(h + 1) * 4096, :] = res.results[c]["outT"].T
    return out
